# version 13
# baseline (speedup 1.0000x reference)
import numpy as np
import ml_dtypes
from contextlib import ExitStack
import concourse.bass as bass
import concourse.mybir as mybir
from concourse.bass_utils import run_bass_kernel_spmd
from concourse.aot_env import allow_ici_all_to_all

F32 = mybir.dt.float32
BF16 = mybir.dt.bfloat16
I32 = mybir.dt.int32
AF = mybir.ActivationFunctionType
ALU = mybir.AluOpType
AX = mybir.AxisListType

B, T, D, DH, H, N = 4, 8192, 1024, 2048, 4096, 16
TCORE = 4096
CH = 8
TC = 512
DT = 8
HC = 32
OC = 16
LN_EPS = 1e-5
QP = 127.0
MIN_RANGE = 1e-6

ENGS = ("sync", "tensor", "vector", "scalar", "gpsimd")


class _StopBuild(Exception):
    pass


def build_nc(debug=(), upto="all"):
    nc = bass.Bass("TRN2", target_bir_lowering=False)

    def din(name, shape, dt=F32):
        return nc.dram_tensor(name, shape, dt, kind="ExternalInput")

    xx = din("xx", [TCORE + 3, D])
    W_in = din("W_in", [D, DH]); b_in = din("b_in", [1, DH])
    conv_w = din("conv_w", [D, 4]); conv_b = din("conv_b", [1, D])
    W_bc = din("W_bc", [D, 2 * N]); b_bc = din("b_bc", [1, 2 * N])
    W_dt = din("W_dt", [D, D]); b_dt = din("b_dt", [1, D])
    A_log_sl = din("A_log_sl", [128, N])
    Dvec = din("Dvec", [1, D])
    W_out = din("W_out", [D, D]); b_out = din("b_out", [1, D])
    ln_g = din("ln_g", [1, D]); ln_b = din("ln_b", [1, D])
    w1 = din("w1", [H, D]); b1 = din("b1", [1, H])
    w2 = din("w2", [D, H]); b2 = din("b2", [1, D])
    ident = din("ident", [128, 128])
    ones_r = din("ones_r", [1, 128])
    selBC = din("selBC", [64, 32 * 128], BF16)
    leadmask = din("leadmask", [128, 1])

    out_t = nc.dram_tensor("out", [TCORE, D], F32, kind="ExternalOutput")
    d_dbg = {}

    def dbg(name, shape):
        if name in debug:
            d_dbg[name] = nc.dram_tensor("dbg_" + name, shape, F32, kind="ExternalOutput")
            return d_dbg[name]
        return None

    xn_hbm = nc.dram_tensor("xn_hbm", [DT, 128, TCORE], F32)
    gate_hbm = nc.dram_tensor("gate_hbm", [DT, 128, TCORE], F32)
    h_hbm = nc.dram_tensor("h_hbm", [HC, 128, TCORE], F32)
    mamba_hbm = nc.dram_tensor("mamba_hbm", [DT, 128, TCORE], F32)
    w1qT_hbm = nc.dram_tensor("w1qT_hbm", [DT, 128, H], BF16)
    w2qT_hbm = nc.dram_tensor("w2qT_hbm", [HC, 128, D], BF16)
    a2a_in = [nc.dram_tensor(f"a2a_in{c}", [8, 2, 128, TC], F32) for c in range(CH)]
    a2a_out = [nc.dram_tensor(f"a2a_out{c}", [8, 2, 128, TC], F32) for c in range(CH)]
    bc_in = [nc.dram_tensor(f"bc_in{c}", [32, TC], F32) for c in range(CH)]
    bc_out = [nc.dram_tensor(f"bc_out{c}", [8, 32, TC], F32) for c in range(CH)]
    ya_in = [nc.dram_tensor(f"ya_in{c}", [8, 128, TC], F32) for c in range(CH)]
    ya_out = [nc.dram_tensor(f"ya_out{c}", [8, 128, TC], F32) for c in range(CH)]
    ar1_in = nc.dram_tensor("ar1_in", [128, 1], F32)
    ar1_out = nc.dram_tensor("ar1_out", [128, 1], F32)
    ar2_in = nc.dram_tensor("ar2_in", [128, 1], F32)
    ar2_out = nc.dram_tensor("ar2_out", [128, 1], F32)

    ops = {e: [] for e in ENGS}
    ctr = {}

    es = ExitStack()
    es.__enter__()
    try:
        def sbuf(name, shape, dt=F32):
            return es.enter_context(nc.sbuf_tensor(name, shape, dt))

        c_bin = sbuf("c_bin", [128, OC])
        c_convw = sbuf("c_convw", [128, DT * 4])
        c_convb = sbuf("c_convb", [128, DT])
        c_bbc = sbuf("c_bbc", [32, 1])
        c_nbdt = sbuf("c_nbdt", [128, DT])
        c_negA = sbuf("c_negA", [128, N])
        c_D = sbuf("c_D", [128, DT])
        c_bout = sbuf("c_bout", [128, DT])
        c_lng = sbuf("c_lng", [128, DT])
        c_lnb = sbuf("c_lnb", [128, DT])
        c_b1 = sbuf("c_b1", [128, HC])
        c_b2 = sbuf("c_b2", [128, DT])
        c_id = sbuf("c_id", [128, 128])
        c_onesr = sbuf("c_onesr", [1, 128])
        c_ones128 = sbuf("c_ones128", [128, 128])
        c_sel = sbuf("c_sel", [64, 32 * 128], BF16)
        c_lead = sbuf("c_lead", [128, 1])
        s_sc = sbuf("s_sc", [128, 8])
        s_b1s = sbuf("s_b1s", [128, HC])
        s_b2s = sbuf("s_b2s", [128, DT])
        a_amax = sbuf("a_amax", [128, 4])
        a_stat = sbuf("a_stat", [128, 32])
        s_state = sbuf("s_state", [128, 4 * N])
        g_tr = sbuf("g_tr", [1, 128])

        pb = [es.enter_context(nc.psum_tensor(f"pb{i}", [128, 512], F32))
              for i in range(8)]

        sems = {}
        for s in ("sLD", "sLE", "sST", "sPE", "sSC", "sVE", "sCC", "sWT", "sWB", "sWX"):
            sems[s] = es.enter_context(nc.semaphore(s))
            ctr[s] = 0

        def w8(eng, s, v=None):
            v = ctr[s] if v is None else v
            if v > 0:
                ops[eng].append(lambda e, h=sems[s], v=v: e.wait_ge(h, v))

        def dma(eng, s, o, i, slow=False):
            ctr[s] += 16
            ops[eng].append(lambda e, o=o, i=i, h=sems[s], slow=slow: e.dma_start(
                out=o, in_=i, allow_slow_non_contiguous=slow).then_inc(h, 16))
            return ctr[s]

        def tag(eng, s):
            ctr[s] += 1
            last = ops[eng].pop()
            ops[eng].append(lambda e, last=last, h=sems[s]: last(e).then_inc(h, 1))
            return ctr[s]

        def op(eng, fn):
            ops[eng].append(fn)

        def dbg_quiesce():
            for e_ in ("vector", "scalar", "tensor"):
                w8(e_, "sST")

        def dr(eng):
            ops[eng].append(lambda e: e.drain())

        def v3(t, a, bdim):
            return t[:, :].rearrange("p (a b) -> p a b", b=bdim)

        phase_es = [None]

        def phase_begin():
            if phase_es[0] is not None:
                phase_es[0].close()
            phase_es[0] = ExitStack()
            w8("sync", "sSC")
            w8("sync", "sVE")
            w8("sync", "sPE")
            w8("sync", "sST")

        def psbuf(name, shape, dt=F32):
            return phase_es[0].enter_context(nc.sbuf_tensor(name, shape, dt))

        def _emit():
            def ld_pm(dst, src, ncols):
                return dma("sync", "sLD", dst[:, 0:ncols],
                           bass.AP(src, 0, [[1, 128], [128, ncols]]), slow=True)

            ld_pm(c_bin, b_in, OC)
            ld_pm(c_convb, conv_b, DT)
            ld_pm(c_nbdt, b_dt, DT)
            ld_pm(c_D, Dvec, DT)
            ld_pm(c_bout, b_out, DT)
            ld_pm(c_lng, ln_g, DT)
            ld_pm(c_lnb, ln_b, DT)
            ld_pm(c_b1, b1, HC)
            ld_pm(c_b2, b2, DT)
            dma("sync", "sLD", c_bbc[:, 0:1], bass.AP(b_bc, 0, [[1, 32], [1, 1]]), slow=True)
            dma("sync", "sLD", v3(c_convw, DT, 4),
                bass.AP(conv_w, 0, [[4, 128], [512, DT], [1, 4]]))
            dma("sync", "sLD", c_negA[:, :], A_log_sl[:, :])
            dma("sync", "sLD", c_id[:, :], ident[:, :])
            dma("sync", "sLD", c_onesr[:, :], ones_r[:, :])
            dma("sync", "sLD", c_sel[:, :], selBC[:, :])
            vld = dma("sync", "sLD", c_lead[:, :], leadmask[:, :])
            w8("scalar", "sLD", vld)
            op("scalar", lambda e: e.activation(out=c_negA[:, :], in_=c_negA[:, :], func=AF.Exp))
            tag("scalar", "sSC")
            w8("vector", "sLD", vld)
            op("vector", lambda e: e.tensor_scalar_mul(c_nbdt[:, :], c_nbdt[:, :], -1.0))
            op("vector", lambda e: e.memset(a_amax[:, 0:4], 0.0))
            op("vector", lambda e: e.memset(c_ones128[:, :], 1.0))
            op("vector", lambda e: e.memset(s_state[:, :], 0.0))
            tag("vector", "sVE")
            if upto == "p0":
                raise _StopBuild

            phase_begin()
            q_w = psbuf("q_w", [128, 32 * 1024])
            q_asm = psbuf("q_asm", [128, H], BF16)
            q_i32 = psbuf("q_i32", [128, 128], I32)

            def prep_w(which):
                wsrc, rows, cols = (w1, H, D) if which == 0 else (w2, D, H)
                rchunks = rows // 128
                scol = 4 + which
                wscol = 6 + which
                dsthbm = w1qT_hbm if which == 0 else w2qT_hbm
                nout = cols // 128
                vw = dma("sync", "sLD", q_w[:, 0:rchunks * cols],
                         bass.AP(wsrc, 0, [[cols, 128], [128 * cols, rchunks], [1, cols]]))
                w8("vector", "sLD", vw)
                op("vector", lambda e, rchunks=rchunks, cols=cols: e.tensor_reduce(
                    a_amax[:, 0:1], q_w[:, 0:rchunks * cols], AX.X, ALU.max,
                    apply_absolute_value=True))
                vve = tag("vector", "sVE")
                w8("sync", "sVE", vve)
                vtr = dma("sync", "sWT", g_tr[0:1, 0:128], a_amax[:, 0:1])
                w8("tensor", "sWT", vtr)
                op("tensor", lambda e: e.matmul(pb[6][:, 0:128], c_onesr[0:1, :], g_tr[0:1, :],
                                                start=True, stop=True))
                vpe = tag("tensor", "sPE")
                w8("vector", "sPE", vpe)
                op("vector", lambda e, wscol=wscol: e.tensor_reduce(
                    s_sc[:, wscol:wscol + 1], pb[6][:, 0:128], AX.X, ALU.max))
                dr("vector")
                op("vector", lambda e, wscol=wscol: e.tensor_scalar_max(
                    s_sc[:, wscol:wscol + 1], s_sc[:, wscol:wscol + 1], MIN_RANGE))
                dr("vector")
                op("vector", lambda e, scol=scol, wscol=wscol: e.reciprocal(
                    s_sc[:, scol:scol + 1], s_sc[:, wscol:wscol + 1]))
                dr("vector")
                op("vector", lambda e, scol=scol: e.tensor_scalar_mul(
                    s_sc[:, scol:scol + 1], s_sc[:, scol:scol + 1], QP))
                op("vector", lambda e, wscol=wscol: e.tensor_scalar_mul(
                    s_sc[:, wscol:wscol + 1], s_sc[:, wscol:wscol + 1], 1.0 / QP))
                dr("vector")
                vve0 = tag("vector", "sVE")
                if upto == "prep_amax":
                    raise _StopBuild
                evac_done = {}
                for ocq in range(nout):
                    for rc in range(rchunks):
                        bank = pb[rc % 2]
                        w8("tensor", "sVE", evac_done.get(rc % 2, vve0))
                        op("tensor", lambda e, rc=rc, ocq=ocq, bank=bank, cols=cols: e.transpose(
                            bank[:, 0:128],
                            q_w[:, rc * cols + ocq * 128: rc * cols + ocq * 128 + 128],
                            c_id[:, :]))
                        vt = tag("tensor", "sPE")
                        w8("vector", "sPE", vt)
                        op("vector", lambda e, bank=bank, scol=scol: e.tensor_scalar_mul(
                            q_i32[:, :], bank[:, 0:128], s_sc[:, scol:scol + 1]))
                        dr("vector")
                        op("vector", lambda e, rc=rc: e.tensor_copy(
                            out=q_asm[:, rc * 128: rc * 128 + 128], in_=q_i32[:, :]))
                        evac_done[rc % 2] = tag("vector", "sVE")
                    w8("sync", "sVE", evac_done[(rchunks - 1) % 2])
                    vst = dma("sync", "sST", dsthbm[ocq, :, 0:rows], q_asm[:, 0:rows])
                    w8("vector", "sST", vst)
                    if upto == "prep_q1":
                        raise _StopBuild

            prep_w(0)
            w8("sync", "sVE")
            prep_w(1)
            if upto == "prep":
                raise _StopBuild

            phase_begin()
            a_xtm = psbuf("a_xtm", [128, 4 * D])
            a_xntm = psbuf("a_xntm", [128, 4 * D])
            a_xnfm = psbuf("a_xnfm", [128, DT * TC])
            a_xpad = psbuf("a_xpad", [128, DT * (TC + 4)])
            a_carry = psbuf("a_carry", [128, DT * 4])
            a_out1 = psbuf("a_out1", [128, DT * TC])
            a_xconv = psbuf("a_xconv", [128, DT * TC])
            w_a = psbuf("w_a", [128, DT * DH])

            vWa = dma("sync", "sWB", w_a[:, :],
                      bass.AP(W_in, 0, [[DH, 128], [128 * DH, DT], [1, DH]]))

            dbg_xn = dbg("xn", [DT, 128, TCORE])
            dbg_xconv = dbg("xconv", [CH, DT, 128, TC])
            dbg_dtneg = dbg("dtneg", [CH, DT, 128, TC])
            dbg_bc = dbg("bc", [CH, 32, TC])
            dbg_gate = dbg("gate", [DT, 128, TCORE])
            dbg_mam = dbg("mam", [DT, 128, TCORE])
            dbg_xntm = dbg("xntm", [CH, 128, 4 * D])
            dbg_stat = dbg("stat", [CH, 128, 32])

            def ln_chunk(nt, xoff):
                w8("sync", "sSC")
                v = dma("sync", "sLD", v3(a_xtm, 4, D)[:, 0:nt, :],
                        bass.AP(xx, xoff * D, [[D, 128], [128 * D, nt], [1, D]]))
                w8("vector", "sLD", v)
                op("vector", lambda e, nt=nt: e.tensor_reduce(
                    a_stat[:, 0:nt], v3(a_xtm, 4, D)[:, 0:nt, :], AX.X, ALU.add))
                tag("vector", "sVE")
                w8("scalar", "sLD", v)
                for tt in range(nt):
                    op("scalar", lambda e, tt=tt: e.activation(
                        out=v3(a_xntm, 4, D)[:, tt, :], in_=v3(a_xtm, 4, D)[:, tt, :],
                        func=AF.Square, accum_out=a_stat[:, 8 + tt:9 + tt]))
                vsc = tag("scalar", "sSC")
                w8("vector", "sSC", vsc)
                op("vector", lambda e, nt=nt: e.tensor_scalar_mul(a_stat[:, 16:16 + nt], a_stat[:, 0:nt], 1.0 / D))
                dr("vector")
                op("vector", lambda e, nt=nt: e.tensor_mul(a_stat[:, 24:24 + nt], a_stat[:, 16:16 + nt], a_stat[:, 16:16 + nt]))
                op("vector", lambda e, nt=nt: e.tensor_scalar_mul(a_stat[:, 8:8 + nt], a_stat[:, 8:8 + nt], 1.0 / D))
                dr("vector")
                op("vector", lambda e, nt=nt: e.tensor_sub(a_stat[:, 8:8 + nt], a_stat[:, 8:8 + nt], a_stat[:, 24:24 + nt]))
                dr("vector")
                op("vector", lambda e, nt=nt: e.tensor_scalar_add(a_stat[:, 8:8 + nt], a_stat[:, 8:8 + nt], LN_EPS))
                vve = tag("vector", "sVE")
                w8("scalar", "sVE", vve)
                op("scalar", lambda e, nt=nt: e.activation(
                    out=a_stat[:, 28:28 + nt], in_=a_stat[:, 8:8 + nt], func=AF.Sqrt))
                vsc = tag("scalar", "sSC")
                w8("vector", "sSC", vsc)
                op("vector", lambda e, nt=nt: e.reciprocal(a_stat[:, 24:24 + nt], a_stat[:, 28:28 + nt]))
                dr("vector")
                op("vector", lambda e, nt=nt: e.tensor_mul(a_stat[:, 16:16 + nt], a_stat[:, 16:16 + nt], a_stat[:, 24:24 + nt]))
                dr("vector")
                op("vector", lambda e, nt=nt: e.tensor_scalar_mul(a_stat[:, 16:16 + nt], a_stat[:, 16:16 + nt], -1.0))
                vve = tag("vector", "sVE")
                w8("scalar", "sVE", vve)
                for tt in range(nt):
                    op("scalar", lambda e, tt=tt: e.activation(
                        out=v3(a_xntm, 4, D)[:, tt, :], in_=v3(a_xtm, 4, D)[:, tt, :], func=AF.Identity,
                        scale=a_stat[:, 24 + tt:25 + tt], bias=a_stat[:, 16 + tt:17 + tt]))
                return tag("scalar", "sSC")

            vsc = ln_chunk(1, 0)
            w8("tensor", "sSC", vsc)
            w8("tensor", "sWB", vWa)
            for dt_ in range(DT):
                op("tensor", lambda e, dt_=dt_: e.transpose(
                    pb[4][:, dt_ * 4: dt_ * 4 + 3],
                    v3(a_xntm, 4, D)[0:3, 0, dt_ * 128:(dt_ + 1) * 128], c_id[0:3, 0:3]))
            vpe = tag("tensor", "sPE")
            w8("vector", "sPE", vpe)
            op("vector", lambda e: e.tensor_copy(out=a_carry[:, 0:DT * 4], in_=pb[4][:, 0:DT * 4]))
            vve = tag("vector", "sVE")
            w8("tensor", "sVE", vve)
            for oc_ in range(DT):
                for kt in range(DT):
                    op("tensor", lambda e, oc_=oc_, kt=kt: e.matmul(
                        pb[5][:, oc_ * 4: oc_ * 4 + 3],
                        v3(w_a, DT, DH)[:, kt, oc_ * 128:(oc_ + 1) * 128],
                        a_carry[:, kt * 4:kt * 4 + 3],
                        start=(kt == 0 and oc_ == 0), stop=(kt == DT - 1 and oc_ == DT - 1)))
            vpe = tag("tensor", "sPE")
            w8("scalar", "sPE", vpe)
            for oc_ in range(DT):
                op("scalar", lambda e, oc_=oc_: e.activation(
                    out=v3(a_xpad, DT, TC + 4)[:, oc_, 0:3], in_=pb[5][:, oc_ * 4:oc_ * 4 + 3],
                    func=AF.Identity, bias=c_bin[:, oc_:oc_ + 1]))
            vsc_lead = tag("scalar", "sSC")
            w8("vector", "sSC", vsc_lead)
            for dt_ in range(DT):
                op("vector", lambda e, dt_=dt_: e.tensor_scalar_mul(
                    v3(a_carry, DT, 4)[:, dt_, 0:3], v3(a_xpad, DT, TC + 4)[:, dt_, 0:3],
                    c_lead[:, 0:1]))
            tag("vector", "sVE")
            evac = {0: ("sVE", ctr["sVE"]), 1: ("sSC", vsc_lead)}
            if upto == "lead":
                raise _StopBuild

            for c in range(CH):
                st_prev = ctr["sST"]
                vsc = ln_chunk(4, 3 + c * TC)
                if dbg_xntm is not None:
                    w8("sync", "sSC", vsc)
                    dma("sync", "sST", bass.AP(dbg_xntm, c * 128 * 4 * D, [[4 * D, 128], [1, 4 * D]]),
                        a_xntm[:, :])
                    dma("sync", "sST", bass.AP(dbg_stat, c * 128 * 32, [[32, 128], [1, 32]]),
                        a_stat[:, :])
                    dbg_quiesce()
                w8("vector", "sSC", vsc)
                op("vector", lambda e: e.tensor_reduce(
                    a_stat[:, 0:4], v3(a_xntm, 4, D)[:, :, :], AX.X, ALU.max, apply_absolute_value=True))
                dr("vector")
                op("vector", lambda e: e.tensor_reduce(
                    a_stat[:, 4:5], a_stat[:, 0:4], AX.X, ALU.max))
                dr("vector")
                op("vector", lambda e: e.tensor_max(a_amax[:, 2:3], a_amax[:, 2:3], a_stat[:, 4:5]))
                tag("vector", "sVE")
                w8("vector", "sST", st_prev)
                w8("tensor", "sSC", vsc)
                for g in range(8):
                    bank = pb[4 + (g % 2)]
                    w8("tensor", *evac[g % 2])
                    for j in range(4):
                        tt, dt_ = divmod(g * 4 + j, DT)
                        op("tensor", lambda e, bank=bank, j=j, tt=tt, dt_=dt_: e.transpose(
                            bank[:, j * 128:(j + 1) * 128],
                            v3(a_xntm, 4, D)[:, tt, dt_ * 128:(dt_ + 1) * 128], c_id[:, :]))
                    vpe = tag("tensor", "sPE")
                    w8("vector", "sPE", vpe)
                    for j in range(4):
                        tt, dt_ = divmod(g * 4 + j, DT)
                        op("vector", lambda e, bank=bank, j=j, tt=tt, dt_=dt_: e.tensor_copy(
                            out=v3(a_xnfm, DT, TC)[:, dt_, tt * 128:(tt + 1) * 128],
                            in_=bank[:, j * 128:(j + 1) * 128]))
                    evac[g % 2] = ("sVE", tag("vector", "sVE"))
                vve_xn = ctr["sVE"]
                w8("sync", "sVE", vve_xn)
                dma("sync", "sST", bass.AP(xn_hbm, c * TC, [[TCORE, 128], [128 * TCORE, DT], [1, TC]]),
                    v3(a_xnfm, DT, TC)[:, :, :])
                if dbg_xn is not None:
                    dma("sync", "sST", bass.AP(dbg_xn, c * TC, [[TCORE, 128], [128 * TCORE, DT], [1, TC]]),
                        v3(a_xnfm, DT, TC)[:, :, :])

                w8("tensor", "sVE", vve_xn)
                evp = {}
                for oc_ in range(OC):
                    bank = pb[oc_ % 2]
                    if oc_ >= 2:
                        w8("tensor", "sSC", evp[oc_ % 2])
                    for kt in range(DT):
                        op("tensor", lambda e, oc_=oc_, kt=kt, bank=bank: e.matmul(
                            bank[:, :], v3(w_a, DT, DH)[:, kt, oc_ * 128:(oc_ + 1) * 128],
                            v3(a_xnfm, DT, TC)[:, kt, :], start=(kt == 0), stop=(kt == DT - 1)))
                    vpe = tag("tensor", "sPE")
                    w8("scalar", "sPE", vpe)
                    if oc_ == 0:
                        w8("scalar", "sST", st_prev)
                    if oc_ < DT:
                        op("scalar", lambda e, oc_=oc_, bank=bank: e.activation(
                            out=v3(a_xpad, DT, TC + 4)[:, oc_, 3:3 + TC], in_=bank[:, :],
                            func=AF.Identity, bias=c_bin[:, oc_:oc_ + 1]))
                    else:
                        op("scalar", lambda e, oc_=oc_, bank=bank: e.activation(
                            out=v3(a_out1, DT, TC)[:, oc_ - DT, :], in_=bank[:, :],
                            func=AF.Silu, bias=c_bin[:, oc_:oc_ + 1]))
                    evp[oc_ % 2] = tag("scalar", "sSC")
                vsc_proj = ctr["sSC"]
                w8("sync", "sSC", vsc_proj)
                dma("sync", "sST", bass.AP(gate_hbm, c * TC, [[TCORE, 128], [128 * TCORE, DT], [1, TC]]),
                    v3(a_out1, DT, TC)[:, :, :])
                if dbg_gate is not None:
                    dma("sync", "sST", bass.AP(dbg_gate, c * TC, [[TCORE, 128], [128 * TCORE, DT], [1, TC]]),
                        v3(a_out1, DT, TC)[:, :, :])

                w8("vector", "sSC", vsc_proj)
                w8("vector", "sST", st_prev)
                for dt_ in range(DT):
                    op("vector", lambda e, dt_=dt_: e.tensor_copy(
                        out=v3(a_xpad, DT, TC + 4)[:, dt_, 0:3], in_=v3(a_carry, DT, 4)[:, dt_, 0:3]))
                for dt_ in range(DT):
                    op("vector", lambda e, dt_=dt_: e.tensor_scalar_mul(
                        v3(a_xconv, DT, TC)[:, dt_, :], v3(a_xpad, DT, TC + 4)[:, dt_, 0:TC],
                        v3(c_convw, DT, 4)[:, dt_, 0:1]))
                    for k in range(1, 4):
                        op("vector", lambda e, dt_=dt_, k=k: e.scalar_tensor_tensor(
                            out=v3(a_xconv, DT, TC)[:, dt_, :],
                            in0=v3(a_xpad, DT, TC + 4)[:, dt_, k:k + TC],
                            scalar=v3(c_convw, DT, 4)[:, dt_, k:k + 1],
                            in1=v3(a_xconv, DT, TC)[:, dt_, :],
                            op0=ALU.mult, op1=ALU.add))
                    op("vector", lambda e, dt_=dt_: e.tensor_copy(
                        out=v3(a_carry, DT, 4)[:, dt_, 0:3],
                        in_=v3(a_xpad, DT, TC + 4)[:, dt_, TC:3 + TC]))
                vve = tag("vector", "sVE")
                w8("scalar", "sVE", vve)
                for dt_ in range(DT):
                    op("scalar", lambda e, dt_=dt_: e.activation(
                        out=v3(a_xconv, DT, TC)[:, dt_, :], in_=v3(a_xconv, DT, TC)[:, dt_, :],
                        func=AF.Silu, bias=c_convb[:, dt_:dt_ + 1]))
                vsc_conv = tag("scalar", "sSC")
                w8("sync", "sSC", vsc_conv)
                dma("sync", "sST", bass.AP(a2a_in[c], 128 * TC, [[TC, 128], [2 * 128 * TC, DT], [1, TC]]),
                    v3(a_xconv, DT, TC)[:, :, :])
                if dbg_xconv is not None:
                    dma("sync", "sST", bass.AP(dbg_xconv, c * DT * 128 * TC, [[TC, 128], [128 * TC, DT], [1, TC]]),
                        v3(a_xconv, DT, TC)[:, :, :])
            if upto == "a1":
                raise _StopBuild

            def global_scale(amax_col, arin, arout, rcol, sccol):
                w8("sync", "sVE")
                dma("sync", "sST", arin[:, :], amax_col)
                w8("gpsimd", "sST")
                ctr["sCC"] += 1
                op("gpsimd", lambda e, arin=arin, arout=arout: e.collective_compute(
                    "AllReduce", ALU.max, ins=[arin[:, :]], outs=[arout[:, :]],
                    replica_groups=[list(range(8))]).then_inc(sems["sCC"], 1))
                w8("sync", "sCC")
                vldx = dma("sync", "sWT", g_tr[0:1, 0:128], arout[:, :])
                w8("tensor", "sWT", vldx)
                op("tensor", lambda e: e.matmul(pb[6][:, 0:128], c_onesr[0:1, :], g_tr[0:1, :],
                                                start=True, stop=True))
                vpe = tag("tensor", "sPE")
                w8("vector", "sPE", vpe)
                op("vector", lambda e, sccol=sccol: e.tensor_reduce(
                    s_sc[:, sccol:sccol + 1], pb[6][:, 0:128], AX.X, ALU.max))
                dr("vector")
                op("vector", lambda e, sccol=sccol: e.tensor_scalar_max(
                    s_sc[:, sccol:sccol + 1], s_sc[:, sccol:sccol + 1], MIN_RANGE))
                dr("vector")
                op("vector", lambda e, rcol=rcol, sccol=sccol: e.reciprocal(
                    s_sc[:, rcol:rcol + 1], s_sc[:, sccol:sccol + 1]))
                dr("vector")
                op("vector", lambda e, rcol=rcol: e.tensor_scalar_mul(
                    s_sc[:, rcol:rcol + 1], s_sc[:, rcol:rcol + 1], QP))
                op("vector", lambda e, sccol=sccol: e.tensor_scalar_mul(
                    s_sc[:, sccol:sccol + 1], s_sc[:, sccol:sccol + 1], 1.0 / QP))
                dr("vector")
                tag("vector", "sVE")

            global_scale(a_amax[:, 2:3], ar1_in, ar1_out, 0, 3)
            op("vector", lambda e: e.tensor_mul(s_sc[:, 1:2], s_sc[:, 6:7], s_sc[:, 3:4]))
            dr("vector")
            op("vector", lambda e: e.tensor_scalar_mul(s_b1s[:, :], c_b1[:, :], s_sc[:, 1:2]))
            tag("vector", "sVE")
            if upto == "ar1":
                raise _StopBuild

            phase_begin()
            a_xc2 = psbuf("a_xc2", [128, DT * TC])
            a_out2 = psbuf("a_out2", [128, DT * TC])
            a_bc = psbuf("a_bc", [32, TC])
            w_b = psbuf("w_b", [128, DT * D])
            w_c = psbuf("w_c", [128, DT * 2 * N])

            dma("sync", "sWB", w_b[:, :],
                bass.AP(W_dt, 0, [[D, 128], [128 * D, DT], [1, D]]))
            vWb = dma("sync", "sWB", w_c[:, :],
                      bass.AP(W_bc, 0, [[2 * N, 128], [128 * 2 * N, DT], [1, 2 * N]]))

            for c in range(CH):
                w8("sync", "sSC")
                w8("sync", "sVE")
                w8("sync", "sST")
                vxc = dma("sync", "sLD", v3(a_xc2, DT, TC)[:, :, :],
                          bass.AP(a2a_in[c], 128 * TC, [[TC, 128], [2 * 128 * TC, DT], [1, TC]]))
                w8("tensor", "sLD", vxc)
                w8("tensor", "sWB", vWb)
                evd = {}
                for oc_ in range(DT):
                    bank = pb[oc_ % 2]
                    if oc_ >= 2:
                        w8("tensor", "sSC", evd[oc_ % 2])
                    for kt in range(DT):
                        op("tensor", lambda e, oc_=oc_, kt=kt, bank=bank: e.matmul(
                            bank[:, :], v3(w_b, DT, D)[:, kt, oc_ * 128:(oc_ + 1) * 128],
                            v3(a_xc2, DT, TC)[:, kt, :], start=(kt == 0), stop=(kt == DT - 1)))
                    vpe = tag("tensor", "sPE")
                    w8("scalar", "sPE", vpe)
                    op("scalar", lambda e, oc_=oc_, bank=bank: e.activation(
                        out=v3(a_out2, DT, TC)[:, oc_, :], in_=bank[:, :],
                        func=AF.Sigmoid, scale=-1.0, bias=c_nbdt[:, oc_:oc_ + 1]))
                    op("scalar", lambda e, oc_=oc_: e.activation(
                        out=v3(a_out2, DT, TC)[:, oc_, :], in_=v3(a_out2, DT, TC)[:, oc_, :],
                        func=AF.Ln))
                    evd[oc_ % 2] = tag("scalar", "sSC")
                bank = pb[2]
                for kt in range(DT):
                    op("tensor", lambda e, kt=kt, bank=bank: e.matmul(
                        bank[0:32, :], v3(w_c, DT, 2 * N)[:, kt, :],
                        v3(a_xc2, DT, TC)[:, kt, :], start=(kt == 0), stop=(kt == DT - 1)))
                vpe_bc = tag("tensor", "sPE")
                w8("scalar", "sPE", vpe_bc)
                op("scalar", lambda e, bank=bank: e.activation(
                    out=a_bc[:, :], in_=bank[0:32, :], func=AF.Identity, bias=c_bbc[:, 0:1]))
                vsc_all = tag("scalar", "sSC")
                w8("vector", "sSC", vsc_all)
                op("vector", lambda e: e.tensor_scalar_max(a_out2[:, :], a_out2[:, :], -80.0))
                vve_dt = tag("vector", "sVE")
                w8("sync", "sVE", vve_dt)
                dma("sync", "sST", bass.AP(a2a_in[c], 0, [[TC, 128], [2 * 128 * TC, DT], [1, TC]]),
                    v3(a_out2, DT, TC)[:, :, :])
                dma("sync", "sST", bc_in[c][:, :], a_bc[:, :])
                if dbg_dtneg is not None:
                    dma("sync", "sST", bass.AP(dbg_dtneg, c * DT * 128 * TC, [[TC, 128], [128 * TC, DT], [1, TC]]),
                        v3(a_out2, DT, TC)[:, :, :])
                if dbg_bc is not None:
                    dma("sync", "sST", bass.AP(dbg_bc, c * 32 * TC, [[TC, 32], [1, TC]]), a_bc[:, :])
                vst = ctr["sST"]
                w8("gpsimd", "sST", vst)
                ctr["sCC"] += 1
                op("gpsimd", lambda e, c=c: e.collective_compute(
                    "AllToAll", ALU.bypass, ins=[a2a_in[c][:, :, :, :]],
                    outs=[a2a_out[c][:, :, :, :]], replica_groups=[list(range(8))]).then_inc(sems["sCC"], 1))
                ctr["sCC"] += 1
                op("gpsimd", lambda e, c=c: e.collective_compute(
                    "AllGather", ALU.bypass, ins=[bc_in[c][:, :]],
                    outs=[bc_out[c][:, :, :]], replica_groups=[list(range(8))]).then_inc(sems["sCC"], 1))
            vcc_a2 = ctr["sCC"]
            if upto == "a2":
                raise _StopBuild

            phase_begin()
            f_xn = psbuf("f_xn", [128, DT * TC])
            f_xq = psbuf("f_xq", [128, DT * TC], BF16)
            f_i32 = psbuf("f_i32", [128, TC], I32)
            f_h = psbuf("f_h", [128, 8 * TC])
            w_q1 = psbuf("w_q1", [128, DT * H], BF16)

            vwq = dma("sync", "sWB", w_q1[:, :],
                      bass.AP(w1qT_hbm, 0, [[H, 128], [128 * H, DT], [1, H]]))
            dbg_hff = dbg("hff", [HC, 128, TCORE])
            dbg_sc = dbg("sc", [128, 8])
            vpe_ffn_prev = 0
            vve_q_prev = 0
            for c in range(CH):
                stp = ctr["sST"]
                w8("sync", "sVE", vve_q_prev)
                vx = dma("sync", "sLD", v3(f_xn, DT, TC)[:, :, :],
                         bass.AP(xn_hbm, c * TC, [[TCORE, 128], [128 * TCORE, DT], [1, TC]]))
                w8("vector", "sLD", vx)
                w8("vector", "sPE", vpe_ffn_prev)
                for dt_ in range(DT):
                    op("vector", lambda e, dt_=dt_: e.tensor_scalar_mul(
                        f_i32[:, :], v3(f_xn, DT, TC)[:, dt_, :], s_sc[:, 0:1]))
                    op("vector", lambda e, dt_=dt_: e.tensor_copy(
                        out=v3(f_xq, DT, TC)[:, dt_, :], in_=f_i32[:, :]))
                vve_q = tag("vector", "sVE")
                vve_q_prev = vve_q
                w8("tensor", "sVE", vve_q)
                w8("tensor", "sWB", vwq)
                evf = {}
                for hc_ in range(HC):
                    bank = pb[hc_ % 2]
                    if hc_ >= 2:
                        w8("tensor", "sSC", evf[hc_ % 2])
                    for kt in range(DT):
                        op("tensor", lambda e, hc_=hc_, kt=kt, bank=bank: e.matmul(
                            bank[:, :], v3(w_q1, DT, H)[:, kt, hc_ * 128:(hc_ + 1) * 128],
                            v3(f_xq, DT, TC)[:, kt, :], start=(kt == 0), stop=(kt == DT - 1)))
                    vpe = tag("tensor", "sPE")
                    w8("scalar", "sPE", vpe)
                    if hc_ % 8 == 0:
                        w8("scalar", "sST", stp if hc_ == 0 else ctr["sST"])
                    op("scalar", lambda e, hc_=hc_, bank=bank: e.activation(
                        out=v3(f_h, 8, TC)[:, hc_ % 8, :], in_=bank[:, :],
                        func=AF.Gelu, scale=s_sc[:, 1:2], bias=s_b1s[:, hc_:hc_ + 1]))
                    evf[hc_ % 2] = tag("scalar", "sSC")
                    if hc_ % 8 == 7:
                        vsc_h = ctr["sSC"]
                        w8("vector", "sSC", vsc_h)
                        op("vector", lambda e: e.tensor_reduce(
                            a_stat[:, 0:8], v3(f_h, 8, TC)[:, :, :], AX.X, ALU.max,
                            apply_absolute_value=True))
                        dr("vector")
                        op("vector", lambda e: e.tensor_reduce(
                            a_stat[:, 8:9], a_stat[:, 0:8], AX.X, ALU.max))
                        dr("vector")
                        op("vector", lambda e: e.tensor_max(a_amax[:, 1:2], a_amax[:, 1:2], a_stat[:, 8:9]))
                        tag("vector", "sVE")
                        w8("sync", "sSC", vsc_h)
                        dma("sync", "sST",
                            bass.AP(h_hbm, (hc_ - 7) * 128 * TCORE + c * TC,
                                    [[TCORE, 128], [128 * TCORE, 8], [1, TC]]),
                            v3(f_h, 8, TC)[:, :, :])
                        if dbg_hff is not None:
                            dma("sync", "sST",
                                bass.AP(dbg_hff, (hc_ - 7) * 128 * TCORE + c * TC,
                                        [[TCORE, 128], [128 * TCORE, 8], [1, TC]]),
                                v3(f_h, 8, TC)[:, :, :])
                            dbg_quiesce()
                        vpe_ffn_prev = ctr["sPE"]

            global_scale(a_amax[:, 1:2], ar2_in, ar2_out, 2, 3)
            op("vector", lambda e: e.tensor_mul(s_sc[:, 3:4], s_sc[:, 7:8], s_sc[:, 3:4]))
            dr("vector")
            op("vector", lambda e: e.tensor_scalar_mul(s_b2s[:, :], c_b2[:, :], s_sc[:, 3:4]))
            tag("vector", "sVE")
            if dbg_sc is not None:
                w8("sync", "sVE")
                dma("sync", "sST", dbg_sc[:, :], s_sc[:, :])
            if upto == "ffn1":
                raise _StopBuild

            phase_begin()
            s_dt = psbuf("s_dt", [128, 2 * TC])
            s_xc = psbuf("s_xc", [128, 2 * TC])
            s_dtx = psbuf("s_dtx", [128, 2 * TC])
            s_bc2 = psbuf("s_bc2", [64, 2 * TC])
            s_bchl = psbuf("s_bchl", [64, 2 * TC], BF16)
            s_dA = psbuf("s_dA", [128, 2 * TC])
            s_u = psbuf("s_u", [128, 2 * TC])
            s_hint = psbuf("s_hint", [128, N * TC])
            s_prod = psbuf("s_prod", [128, N * TC])
            s_y = psbuf("s_y", [128, TC])

            seg_done = {}
            segi = 0
            for hh in range(2):
                for c2 in range(CH):
                    for b_ in range(4):
                        slot = 2 * b_ + hh
                        sp_ = segi % 2
                        o_ = sp_ * TC
                        w8("sync", "sCC", min(2 * (c2 + 1), vcc_a2))
                        ch_ = "sLD" if sp_ == 0 else "sLE"
                        if sp_ in seg_done:
                            w8("sync", "sSC", seg_done[sp_][0])
                            w8("sync", "sVE", seg_done[sp_][1])
                        dma("sync", ch_, s_dt[:, o_:o_ + TC], a2a_out[c2][slot, 0, :, :])
                        dma("sync", ch_, s_xc[:, o_:o_ + TC], a2a_out[c2][slot, 1, :, :])
                        dma("sync", ch_, s_bc2[0:32, o_:o_ + TC], bc_out[c2][slot, :, :])
                        vbc = dma("sync", ch_, s_bc2[32:64, o_:o_ + TC], bc_out[c2][slot, :, :])
                        w8("vector", ch_, vbc)
                        op("vector", lambda e, o_=o_: e.tensor_mul(
                            s_dtx[:, o_:o_ + TC], s_dt[:, o_:o_ + TC], s_xc[:, o_:o_ + TC]))
                        op("vector", lambda e, o_=o_: e.tensor_copy(
                            out=s_bchl[0:32, o_:o_ + TC], in_=s_bc2[0:32, o_:o_ + TC]))
                        op("vector", lambda e, o_=o_: e.tensor_copy(
                            out=s_bchl[32:64, o_:o_ + TC], in_=s_bc2[32:64, o_:o_ + TC]))
                        op("vector", lambda e, o_=o_: e.tensor_sub(
                            s_bchl[32:64, o_:o_ + TC], s_bc2[32:64, o_:o_ + TC], s_bchl[32:64, o_:o_ + TC]))
                        vve_pre = tag("vector", "sVE")
                        vu = {}
                        vscan = {}
                        vprod = {}
                        for n_ in range(N):
                            par = n_ % 2
                            pbB = pb[4 + par]
                            pbC = pb[6 + par]
                            w8("tensor", "sVE", vu.get(par, vve_pre))
                            op("tensor", lambda e, n_=n_, pbB=pbB, o_=o_: e.matmul(
                                pbB[:, :], c_sel[:, n_ * 128:(n_ + 1) * 128], s_bchl[:, o_:o_ + TC],
                                start=True, stop=True))
                            w8("tensor", "sVE", vprod.get(par, vve_pre))
                            op("tensor", lambda e, n_=n_, pbC=pbC, o_=o_: e.matmul(
                                pbC[:, :], c_sel[:, (16 + n_) * 128:(17 + n_) * 128], s_bchl[:, o_:o_ + TC],
                                start=True, stop=True))
                            vpe_n = tag("tensor", "sPE")
                            w8("scalar", "sVE", vscan.get(par, vve_pre))
                            op("scalar", lambda e, n_=n_, par=par, o_=o_: e.activation(
                                out=s_dA[:, par * TC:(par + 1) * TC], in_=s_dt[:, o_:o_ + TC],
                                func=AF.Exp, scale=c_negA[:, n_:n_ + 1]))
                            vscal = tag("scalar", "sSC")
                            w8("vector", "sPE", vpe_n)
                            op("vector", lambda e, par=par, pbB=pbB, o_=o_: e.tensor_mul(
                                s_u[:, par * TC:(par + 1) * TC], s_dtx[:, o_:o_ + TC], pbB[:, :]))
                            vu[par] = tag("vector", "sVE")
                            w8("vector", "sSC", vscal)
                            op("vector", lambda e, n_=n_, b_=b_, par=par: e.tensor_tensor_scan(
                                out=bass.AP(s_hint, n_, [[N * TC, 128], [N, TC]]),
                                data0=s_dA[:, par * TC:(par + 1) * TC],
                                data1=s_u[:, par * TC:(par + 1) * TC],
                                initial=s_state[:, b_ * N + n_: b_ * N + n_ + 1],
                                op0=ALU.mult, op1=ALU.add))
                            vscan[par] = tag("vector", "sVE")
                            op("vector", lambda e, n_=n_, pbC=pbC: e.tensor_mul(
                                v3(s_prod, TC, N)[:, :, n_:n_ + 1],
                                v3(s_hint, TC, N)[:, :, n_:n_ + 1],
                                pbC[:, :]))
                            vprod[par] = tag("vector", "sVE")
                        op("vector", lambda e, b_=b_: e.tensor_copy(
                            out=s_state[:, b_ * N:(b_ + 1) * N],
                            in_=s_hint[:, (TC - 1) * N: TC * N]))
                        w8("vector", "sST")
                        op("vector", lambda e: e.tensor_reduce(
                            s_y[:, :], v3(s_prod, TC, N)[:, :, :], AX.X, ALU.add))
                        vve_y = tag("vector", "sVE")
                        seg_done[sp_] = (ctr["sSC"], vve_y)
                        segi += 1
                        w8("sync", "sVE", vve_y)
                        dma("sync", "sST", ya_in[c2][slot, :, :], s_y[:, :])
                    if hh == 1:
                        vst = ctr["sST"]
                        w8("gpsimd", "sST", vst)
                        ctr["sCC"] += 1
                        op("gpsimd", lambda e, c2=c2: e.collective_compute(
                            "AllToAll", ALU.bypass, ins=[ya_in[c2][:, :, :]],
                            outs=[ya_out[c2][:, :, :]], replica_groups=[list(range(8))]).then_inc(sems["sCC"], 1))

            vcc_scan = ctr["sCC"]
            if upto == "scan":
                raise _StopBuild

            phase_begin()
            p_yfm = psbuf("p_yfm", [128, DT * TC])
            p_aux = psbuf("p_aux", [128, DT * TC])
            p_gate = psbuf("p_gate", [128, DT * TC])
            p_xn = psbuf("p_xn", [128, DT * TC])
            p_res = psbuf("p_res", [128, DT * TC])
            p_tmp = psbuf("p_tmp", [128, DT * TC])
            p_mu = psbuf("p_mu", [128, TC])
            p_rs = psbuf("p_rs", [128, TC])
            w_o = psbuf("w_o", [128, DT * D])

            vWo = dma("sync", "sWB", w_o[:, :],
                      bass.AP(W_out, 0, [[D, 128], [128 * D, DT], [1, D]]))
            pv = {}
            for c in range(CH):
                w8("sync", "sCC", vcc_scan - (CH - 1 - c))
                if c == 0:
                    w8("sync", "sSC")
                    w8("sync", "sVE")
                    w8("sync", "sST")
                w8("sync", "sVE", pv.get("y2", 0))
                dma("sync", "sLD", v3(p_aux, DT, TC)[:, :, :],
                    bass.AP(a2a_in[c], 128 * TC, [[TC, 128], [2 * 128 * TC, DT], [1, TC]]))
                dma("sync", "sLD", v3(p_gate, DT, TC)[:, :, :],
                    bass.AP(gate_hbm, c * TC, [[TCORE, 128], [128 * TCORE, DT], [1, TC]]))
                w8("sync", "sVE", pv.get("res", 0))
                dma("sync", "sLD", v3(p_xn, DT, TC)[:, :, :],
                    bass.AP(xn_hbm, c * TC, [[TCORE, 128], [128 * TCORE, DT], [1, TC]]))
                w8("sync", "sPE", pv.get("wo", 0))
                vxn = dma("sync", "sLD", v3(p_yfm, DT, TC)[:, :, :],
                          bass.AP(ya_out[c], 0, [[TC, 128], [128 * TC, 8], [1, TC]]))
                w8("vector", "sLD", vxn)
                for dt_ in range(DT):
                    op("vector", lambda e, dt_=dt_: e.scalar_tensor_tensor(
                        out=v3(p_tmp, DT, TC)[:, dt_, :], in0=v3(p_aux, DT, TC)[:, dt_, :],
                        scalar=c_D[:, dt_:dt_ + 1], in1=v3(p_yfm, DT, TC)[:, dt_, :],
                        op0=ALU.mult, op1=ALU.add))
                    op("vector", lambda e, dt_=dt_: e.tensor_mul(
                        v3(p_yfm, DT, TC)[:, dt_, :], v3(p_tmp, DT, TC)[:, dt_, :],
                        v3(p_gate, DT, TC)[:, dt_, :]))
                vve_y2 = tag("vector", "sVE")
                pv["y2"] = vve_y2
                w8("tensor", "sVE", vve_y2)
                w8("tensor", "sWB", vWo)
                w8("tensor", "sSC", pv.get("evo1", 0))
                w8("tensor", "sVE", pv.get("st", 0))
                w8("scalar", "sST", pv.get("vst", 0))
                evo = {}
                for oc_ in range(DT):
                    bank = pb[oc_ % 2]
                    if oc_ >= 2:
                        w8("tensor", "sSC", evo[oc_ % 2])
                    for kt in range(DT):
                        op("tensor", lambda e, oc_=oc_, kt=kt, bank=bank: e.matmul(
                            bank[:, :], v3(w_o, DT, D)[:, kt, oc_ * 128:(oc_ + 1) * 128],
                            v3(p_yfm, DT, TC)[:, kt, :], start=(kt == 0), stop=(kt == DT - 1)))
                    vpe = tag("tensor", "sPE")
                    w8("scalar", "sPE", vpe)
                    op("scalar", lambda e, oc_=oc_, bank=bank: e.activation(
                        out=v3(p_tmp, DT, TC)[:, oc_, :], in_=bank[:, :],
                        func=AF.Identity, bias=c_bout[:, oc_:oc_ + 1]))
                    evo[oc_ % 2] = tag("scalar", "sSC")
                pv["evo1"] = max(evo.values())
                pv["wo"] = ctr["sPE"]
                vsc_wo = ctr["sSC"]
                w8("vector", "sSC", vsc_wo)
                for dt_ in range(DT):
                    op("vector", lambda e, dt_=dt_: e.tensor_add(
                        v3(p_res, DT, TC)[:, dt_, :], v3(p_tmp, DT, TC)[:, dt_, :],
                        v3(p_xn, DT, TC)[:, dt_, :]))
                vve_res = tag("vector", "sVE")
                pv["res"] = vve_res
                w8("scalar", "sVE", vve_res)
                for dt_ in range(DT):
                    op("scalar", lambda e, dt_=dt_: e.activation(
                        out=v3(p_tmp, DT, TC)[:, dt_, :], in_=v3(p_res, DT, TC)[:, dt_, :],
                        func=AF.Square))
                vsc_sq = tag("scalar", "sSC")
                w8("tensor", "sSC", vsc_sq)
                for kt in range(DT):
                    op("tensor", lambda e, kt=kt: e.matmul(
                        pb[2][:, :], c_ones128[:, :], v3(p_res, DT, TC)[:, kt, :],
                        start=(kt == 0), stop=(kt == DT - 1)))
                for kt in range(DT):
                    op("tensor", lambda e, kt=kt: e.matmul(
                        pb[3][:, :], c_ones128[:, :], v3(p_tmp, DT, TC)[:, kt, :],
                        start=(kt == 0), stop=(kt == DT - 1)))
                vpe_s = tag("tensor", "sPE")
                w8("vector", "sPE", vpe_s)
                op("vector", lambda e: e.tensor_scalar_mul(p_mu[:, :], pb[2][:, :], 1.0 / D))
                op("vector", lambda e: e.tensor_scalar_mul(p_rs[:, :], pb[3][:, :], 1.0 / D))
                dr("vector")
                op("vector", lambda e: e.tensor_mul(p_tmp[:, 0:TC], p_mu[:, :], p_mu[:, :]))
                dr("vector")
                op("vector", lambda e: e.tensor_sub(p_rs[:, :], p_rs[:, :], p_tmp[:, 0:TC]))
                dr("vector")
                op("vector", lambda e: e.tensor_scalar_add(p_rs[:, :], p_rs[:, :], LN_EPS))
                vve_st = tag("vector", "sVE")
                pv["st"] = vve_st
                w8("scalar", "sVE", vve_st)
                op("scalar", lambda e: e.activation(
                    out=p_tmp[:, 0:TC], in_=p_rs[:, :], func=AF.Sqrt))
                vsc_st0 = tag("scalar", "sSC")
                w8("vector", "sSC", vsc_st0)
                op("vector", lambda e: e.reciprocal(p_rs[:, :], p_tmp[:, 0:TC]))
                vsc_st = tag("vector", "sVE")
                w8("vector", "sVE", vsc_st)
                for dt_ in range(DT):
                    op("vector", lambda e, dt_=dt_: e.tensor_sub(
                        v3(p_res, DT, TC)[:, dt_, :], v3(p_res, DT, TC)[:, dt_, :], p_mu[:, :]))
                    op("vector", lambda e, dt_=dt_: e.tensor_mul(
                        v3(p_res, DT, TC)[:, dt_, :], v3(p_res, DT, TC)[:, dt_, :], p_rs[:, :]))
                vve_n2 = tag("vector", "sVE")
                w8("scalar", "sVE", vve_n2)
                for dt_ in range(DT):
                    op("scalar", lambda e, dt_=dt_: e.activation(
                        out=v3(p_tmp, DT, TC)[:, dt_, :], in_=v3(p_res, DT, TC)[:, dt_, :],
                        func=AF.Identity, scale=c_lng[:, dt_:dt_ + 1], bias=c_lnb[:, dt_:dt_ + 1]))
                vsc_m = tag("scalar", "sSC")
                w8("sync", "sSC", vsc_m)
                dma("sync", "sST", bass.AP(mamba_hbm, c * TC, [[TCORE, 128], [128 * TCORE, DT], [1, TC]]),
                    v3(p_tmp, DT, TC)[:, :, :])
                pv["vst"] = ctr["sST"]
                if dbg_mam is not None:
                    dma("sync", "sST", bass.AP(dbg_mam, c * TC, [[TCORE, 128], [128 * TCORE, DT], [1, TC]]),
                        v3(p_tmp, DT, TC)[:, :, :])
            if upto == "d1":
                raise _StopBuild

            phase_begin()
            g_mam = psbuf("g_mam", [128, DT * TC])
            g_tmp = psbuf("g_tmp", [128, DT * TC])
            g_ttm = psbuf("g_ttm", [128, 4 * D])
            g_xtm = psbuf("g_xtm", [128, 4 * D])
            g_hin = psbuf("g_hin", [128, 8 * TC])
            g_hq = psbuf("g_hq", [128, 8 * TC], BF16)
            g_i32 = psbuf("g_i32", [128, 4 * TC], I32)
            w_q2 = psbuf("w_q2", [128, HC * D], BF16)

            dbg_ffn = dbg("ffn", [DT, 128, TCORE])
            vwq2 = dma("sync", "sWB", w_q2[:, :],
                       bass.AP(w2qT_hbm, 0, [[D, 128], [128 * D, HC], [1, D]]))
            vhq = {}
            vq = {}
            prev = {}
            NG = HC // 4
            for c in range(CH):
                if c == 0:
                    w8("sync", "sSC")
                    w8("sync", "sVE")
                    w8("sync", "sST")
                for grp in range(2):
                    par = grp % 2
                    ch_ = "sLD" if par == 0 else "sLE"
                    if par in vq:
                        w8("sync", "sVE", vq[par])
                    vh = dma("sync", ch_, g_hin[:, par * 4 * TC:(par + 1) * 4 * TC],
                             bass.AP(h_hbm, grp * 4 * 128 * TCORE + c * TC,
                                     [[TCORE, 128], [128 * TCORE, 4], [1, TC]]))
                    prev[("vh", grp)] = vh
                if c > 0:
                    w8("sync", "sVE", prev["vve_tot"])
                vm = dma("sync", "sWT", v3(g_mam, DT, TC)[:, :, :],
                         bass.AP(mamba_hbm, c * TC, [[TCORE, 128], [128 * TCORE, DT], [1, TC]]))
                if c > 0:
                    w8("sync", "sVE", prev["vve_o"])
                vxtm = dma("sync", "sWX", v3(g_xtm, 4, D)[:, :, :],
                           bass.AP(xx, (3 + c * TC) * D, [[D, 128], [128 * D, 4], [1, D]]))
                if c == 0:
                    w8("tensor", "sWB", vwq2)
                else:
                    w8("tensor", "sVE", prev["evt1"])
                    w8("tensor", "sSC", prev["vsc_f"])
                for grp in range(NG):
                    par = grp % 2
                    ch_ = "sLD" if par == 0 else "sLE"
                    if grp < 2:
                        vh = prev[("vh", grp)]
                    else:
                        if par in vq:
                            w8("sync", "sVE", vq[par])
                        vh = dma("sync", ch_, g_hin[:, par * 4 * TC:(par + 1) * 4 * TC],
                                 bass.AP(h_hbm, grp * 4 * 128 * TCORE + c * TC,
                                         [[TCORE, 128], [128 * TCORE, 4], [1, TC]]))
                    w8("vector", ch_, vh)
                    w8("vector", "sPE", vhq.get(par, 0))
                    op("vector", lambda e, par=par: e.tensor_scalar_mul(
                        g_i32[:, :], g_hin[:, par * 4 * TC:(par + 1) * 4 * TC],
                        s_sc[:, 2:3]))
                    op("vector", lambda e, par=par: e.tensor_copy(
                        out=g_hq[:, par * 4 * TC:(par + 1) * 4 * TC], in_=g_i32[:, :]))
                    vve_hq = tag("vector", "sVE")
                    vq[par] = vve_hq
                    w8("tensor", "sVE", vve_hq)
                    for ki in range(4):
                        kt = grp * 4 + ki
                        for oc_ in range(DT):
                            op("tensor", lambda e, kt=kt, ki=ki, oc_=oc_, par=par: e.matmul(
                                pb[oc_][:, :], v3(w_q2, HC, D)[:, kt, oc_ * 128:(oc_ + 1) * 128],
                                g_hq[:, (par * 4 + ki) * TC:(par * 4 + ki + 1) * TC],
                                start=(kt == 0), stop=(kt == HC - 1)))
                    vhq[par] = tag("tensor", "sPE")
                vpe_f = ctr["sPE"]
                w8("scalar", "sPE", vpe_f)
                w8("scalar", "sWT", vm)
                for oc_ in range(DT):
                    op("scalar", lambda e, oc_=oc_: e.activation(
                        out=v3(g_tmp, DT, TC)[:, oc_, :], in_=pb[oc_][:, :],
                        func=AF.Identity, scale=s_sc[:, 3:4], bias=s_b2s[:, oc_:oc_ + 1]))
                vsc_f = tag("scalar", "sSC")
                prev["vsc_f"] = vsc_f
                if dbg_ffn is not None:
                    w8("sync", "sSC", vsc_f)
                    dma("sync", "sST", bass.AP(dbg_ffn, c * TC, [[TCORE, 128], [128 * TCORE, DT], [1, TC]]),
                        v3(g_tmp, DT, TC)[:, :, :])
                    dbg_quiesce()
                w8("vector", "sSC", vsc_f)
                for dt_ in range(DT):
                    op("vector", lambda e, dt_=dt_: e.tensor_add(
                        v3(g_tmp, DT, TC)[:, dt_, :], v3(g_tmp, DT, TC)[:, dt_, :],
                        v3(g_mam, DT, TC)[:, dt_, :]))
                vve_tot = tag("vector", "sVE")
                prev["vve_tot"] = vve_tot
                w8("tensor", "sVE", vve_tot)
                evt = {}
                w8("vector", "sST", prev.get("vst", 0))
                for g in range(8):
                    bank = pb[g % 2]
                    if g >= 2:
                        w8("tensor", "sVE", evt[g % 2])
                    for j in range(4):
                        dt_, tt = divmod(g * 4 + j, 4)
                        op("tensor", lambda e, bank=bank, j=j, tt=tt, dt_=dt_: e.transpose(
                            bank[:, j * 128:(j + 1) * 128],
                            v3(g_tmp, DT, TC)[:, dt_, tt * 128:(tt + 1) * 128], c_id[:, :]))
                    vpe = tag("tensor", "sPE")
                    w8("vector", "sPE", vpe)
                    for j in range(4):
                        dt_, tt = divmod(g * 4 + j, 4)
                        op("vector", lambda e, bank=bank, j=j, tt=tt, dt_=dt_: e.tensor_copy(
                            out=v3(g_ttm, 4, D)[:, tt, dt_ * 128:(dt_ + 1) * 128],
                            in_=bank[:, j * 128:(j + 1) * 128]))
                    evt[g % 2] = tag("vector", "sVE")
                prev["evt1"] = evt[1]
                w8("vector", "sWX", vxtm)
                for tt in range(4):
                    op("vector", lambda e, tt=tt: e.tensor_add(
                        v3(g_ttm, 4, D)[:, tt, :], v3(g_ttm, 4, D)[:, tt, :],
                        v3(g_xtm, 4, D)[:, tt, :]))
                vve_o = tag("vector", "sVE")
                prev["vve_o"] = vve_o
                w8("sync", "sVE", vve_o)
                dma("sync", "sST", bass.AP(out_t, c * TC * D, [[D, 128], [128 * D, 4], [1, D]]),
                    v3(g_ttm, 4, D)[:, :, :])
                prev["vst"] = ctr["sST"]

        try:
            _emit()
        except _StopBuild:
            pass
        if phase_es[0] is not None:
            phase_es[0].close()
            phase_es[0] = None
        w8("sync", "sST")
        w8("sync", "sLD")
        w8("sync", "sLE")

        with nc.Block() as block:
            @block.sync
            def _(eng):
                for f in ops["sync"]:
                    f(eng)

            @block.tensor
            def _(eng):
                for f in ops["tensor"]:
                    f(eng)

            @block.vector
            def _(eng):
                for f in ops["vector"]:
                    f(eng)

            @block.scalar
            def _(eng):
                for f in ops["scalar"]:
                    f(eng)

            @block.gpsimd
            def _(eng):
                with allow_ici_all_to_all():
                    for f in ops["gpsimd"]:
                        f(eng)
    finally:
        es.__exit__(None, None, None)

    return nc


_SEL_CACHE = {}


def _host_consts():
    if "sel" not in _SEL_CACHE:
        sel = np.zeros((64, 32 * 128), np.float32)
        for n_ in range(N):
            sel[n_, n_ * 128:(n_ + 1) * 128] = -1.0
            sel[32 + n_, n_ * 128:(n_ + 1) * 128] = -1.0
            sel[16 + n_, (16 + n_) * 128:(17 + n_) * 128] = 1.0
            sel[48 + n_, (16 + n_) * 128:(17 + n_) * 128] = 1.0
        _SEL_CACHE["sel"] = sel.astype(ml_dtypes.bfloat16)
        _SEL_CACHE["id"] = np.eye(128, dtype=np.float32)
        _SEL_CACHE["onesr"] = np.ones((1, 128), np.float32)
    return _SEL_CACHE


def make_in_maps(inputs):
    x = np.asarray(inputs["L0_attn_x"], np.float32)
    cst = _host_consts()
    common = dict(
        W_in=np.ascontiguousarray(np.asarray(inputs["L0_attn_W_in"], np.float32)),
        b_in=np.asarray(inputs["L0_attn_b_in"], np.float32).reshape(1, DH),
        conv_w=np.ascontiguousarray(np.asarray(inputs["L0_attn_conv_w"], np.float32).reshape(D, 4)),
        conv_b=np.asarray(inputs["L0_attn_conv_b"], np.float32).reshape(1, D),
        W_bc=np.ascontiguousarray(np.asarray(inputs["L0_attn_W_bc"], np.float32)),
        b_bc=np.asarray(inputs["L0_attn_b_bc"], np.float32).reshape(1, 2 * N),
        W_dt=np.ascontiguousarray(np.asarray(inputs["L0_attn_W_dt"], np.float32)),
        b_dt=np.asarray(inputs["L0_attn_b_dt"], np.float32).reshape(1, D),
        Dvec=np.asarray(inputs["L0_attn_D"], np.float32).reshape(1, D),
        W_out=np.ascontiguousarray(np.asarray(inputs["L0_attn_W_out"], np.float32)),
        b_out=np.asarray(inputs["L0_attn_b_out"], np.float32).reshape(1, D),
        ln_g=np.asarray(inputs["L0_attn_ln_g"], np.float32).reshape(1, D),
        ln_b=np.asarray(inputs["L0_attn_ln_b"], np.float32).reshape(1, D),
        w1=np.ascontiguousarray(np.asarray(inputs["L0_ffn_w1"], np.float32)),
        b1=np.asarray(inputs["L0_ffn_b1"], np.float32).reshape(1, H),
        w2=np.ascontiguousarray(np.asarray(inputs["L0_ffn_w2"], np.float32)),
        b2=np.asarray(inputs["L0_ffn_b2"], np.float32).reshape(1, D),
        ident=cst["id"], ones_r=cst["onesr"], selBC=cst["sel"],
    )
    A_log = np.asarray(inputs["L0_attn_A_log"], np.float32)
    in_maps = []
    for k in range(8):
        b_, hh = divmod(k, 2)
        base = hh * TCORE
        if hh == 0:
            lead = np.zeros((3, D), np.float32)
            lm = np.zeros((128, 1), np.float32)
        else:
            lead = x[b_, base - 3:base, :]
            lm = np.ones((128, 1), np.float32)
        xxk = np.concatenate([lead, x[b_, base:base + TCORE, :]], axis=0)
        m = dict(common)
        m["xx"] = np.ascontiguousarray(xxk)
        m["A_log_sl"] = np.ascontiguousarray(A_log[k * 128:(k + 1) * 128, :])
        m["leadmask"] = lm
        in_maps.append(m)
    return in_maps


def kernel(**inputs):
    in_maps = make_in_maps(inputs)
    nc = build_nc()
    res = run_bass_kernel_spmd(nc, in_maps, core_ids=list(range(8)))
    outs = np.stack([np.asarray(r["out"]) for r in res.results])
    return np.ascontiguousarray(outs.reshape(B, T, D).astype(np.float32))


# version 14
# speedup vs baseline: 1.0175x; 1.0175x over previous
import numpy as np
import ml_dtypes
from contextlib import ExitStack
import concourse.bass as bass
import concourse.mybir as mybir
from concourse.bass_utils import run_bass_kernel_spmd
from concourse.aot_env import allow_ici_all_to_all

F32 = mybir.dt.float32
BF16 = mybir.dt.bfloat16
I32 = mybir.dt.int32
AF = mybir.ActivationFunctionType
ALU = mybir.AluOpType
AX = mybir.AxisListType

B, T, D, DH, H, N = 4, 8192, 1024, 2048, 4096, 16
TCORE = 4096
CH = 8
TC = 512
DT = 8
HC = 32
OC = 16
LN_EPS = 1e-5
QP = 127.0
MIN_RANGE = 1e-6

ENGS = ("sync", "tensor", "vector", "scalar", "gpsimd")


class _StopBuild(Exception):
    pass


def build_nc(debug=(), upto="all"):
    nc = bass.Bass("TRN2", target_bir_lowering=False)

    def din(name, shape, dt=F32):
        return nc.dram_tensor(name, shape, dt, kind="ExternalInput")

    xx = din("xx", [TCORE + 3, D])
    W_in = din("W_in", [D, DH]); b_in = din("b_in", [1, DH])
    conv_w = din("conv_w", [D, 4]); conv_b = din("conv_b", [1, D])
    W_bc = din("W_bc", [D, 2 * N]); b_bc = din("b_bc", [1, 2 * N])
    W_dt = din("W_dt", [D, D]); b_dt = din("b_dt", [1, D])
    A_log_sl = din("A_log_sl", [128, N])
    Dvec = din("Dvec", [1, D])
    W_out = din("W_out", [D, D]); b_out = din("b_out", [1, D])
    ln_g = din("ln_g", [1, D]); ln_b = din("ln_b", [1, D])
    w1 = din("w1", [H, D]); b1 = din("b1", [1, H])
    w2 = din("w2", [D, H]); b2 = din("b2", [1, D])
    ident = din("ident", [128, 128])
    ones_r = din("ones_r", [1, 128])
    selBC = din("selBC", [64, 32 * 128], BF16)
    leadmask = din("leadmask", [128, 1])

    out_t = nc.dram_tensor("out", [TCORE, D], F32, kind="ExternalOutput")
    d_dbg = {}

    def dbg(name, shape):
        if name in debug:
            d_dbg[name] = nc.dram_tensor("dbg_" + name, shape, F32, kind="ExternalOutput")
            return d_dbg[name]
        return None

    xn_hbm = nc.dram_tensor("xn_hbm", [DT, 128, TCORE], F32)
    gate_hbm = nc.dram_tensor("gate_hbm", [DT, 128, TCORE], F32)
    h_hbm = nc.dram_tensor("h_hbm", [HC, 128, TCORE], F32)
    mamba_hbm = nc.dram_tensor("mamba_hbm", [DT, 128, TCORE], F32)
    w1qT_hbm = nc.dram_tensor("w1qT_hbm", [DT, 128, H], BF16)
    w2qT_hbm = nc.dram_tensor("w2qT_hbm", [HC, 128, D], BF16)
    a2a_in = [nc.dram_tensor(f"a2a_in{c}", [8, 2, 128, TC], F32) for c in range(CH)]
    a2a_out = [nc.dram_tensor(f"a2a_out{c}", [8, 2, 128, TC], F32) for c in range(CH)]
    bc_in = [nc.dram_tensor(f"bc_in{c}", [32, TC], F32) for c in range(CH)]
    bc_out = [nc.dram_tensor(f"bc_out{c}", [8, 32, TC], F32) for c in range(CH)]
    ya_in = [nc.dram_tensor(f"ya_in{c}", [8, 128, TC], F32) for c in range(CH)]
    ya_out = [nc.dram_tensor(f"ya_out{c}", [8, 128, TC], F32) for c in range(CH)]
    ar1_in = nc.dram_tensor("ar1_in", [128, 1], F32)
    ar1_out = nc.dram_tensor("ar1_out", [128, 1], F32)
    ar2_in = nc.dram_tensor("ar2_in", [128, 1], F32)
    ar2_out = nc.dram_tensor("ar2_out", [128, 1], F32)

    ops = {e: [] for e in ENGS}
    ctr = {}

    es = ExitStack()
    es.__enter__()
    try:
        def sbuf(name, shape, dt=F32):
            return es.enter_context(nc.sbuf_tensor(name, shape, dt))

        c_bin = sbuf("c_bin", [128, OC])
        c_convw = sbuf("c_convw", [128, DT * 4])
        c_convb = sbuf("c_convb", [128, DT])
        c_bbc = sbuf("c_bbc", [32, 1])
        c_nbdt = sbuf("c_nbdt", [128, DT])
        c_negA = sbuf("c_negA", [128, N])
        c_D = sbuf("c_D", [128, DT])
        c_bout = sbuf("c_bout", [128, DT])
        c_lng = sbuf("c_lng", [128, DT])
        c_lnb = sbuf("c_lnb", [128, DT])
        c_b1 = sbuf("c_b1", [128, HC])
        c_b2 = sbuf("c_b2", [128, DT])
        c_id = sbuf("c_id", [128, 128])
        c_onesr = sbuf("c_onesr", [1, 128])
        c_ones128 = sbuf("c_ones128", [128, 128])
        c_sel = sbuf("c_sel", [64, 32 * 128], BF16)
        c_lead = sbuf("c_lead", [128, 1])
        s_sc = sbuf("s_sc", [128, 8])
        s_b1s = sbuf("s_b1s", [128, HC])
        s_b2s = sbuf("s_b2s", [128, DT])
        a_amax = sbuf("a_amax", [128, 4])
        a_stat = sbuf("a_stat", [128, 32])
        s_state = sbuf("s_state", [128, 4 * N])
        g_tr = sbuf("g_tr", [1, 128])

        pb = [es.enter_context(nc.psum_tensor(f"pb{i}", [128, 512], F32))
              for i in range(8)]

        sems = {}
        for s in ("sLD", "sLE", "sST", "sPE", "sSC", "sVE", "sCC", "sWT", "sWB", "sWX"):
            sems[s] = es.enter_context(nc.semaphore(s))
            ctr[s] = 0

        def w8(eng, s, v=None):
            v = ctr[s] if v is None else v
            if v > 0:
                ops[eng].append(lambda e, h=sems[s], v=v: e.wait_ge(h, v))

        def dma(eng, s, o, i, slow=False):
            ctr[s] += 16
            ops[eng].append(lambda e, o=o, i=i, h=sems[s], slow=slow: e.dma_start(
                out=o, in_=i, allow_slow_non_contiguous=slow).then_inc(h, 16))
            return ctr[s]

        def tag(eng, s):
            ctr[s] += 1
            last = ops[eng].pop()
            ops[eng].append(lambda e, last=last, h=sems[s]: last(e).then_inc(h, 1))
            return ctr[s]

        def op(eng, fn):
            ops[eng].append(fn)

        def dbg_quiesce():
            for e_ in ("vector", "scalar", "tensor"):
                w8(e_, "sST")

        def dr(eng):
            ops[eng].append(lambda e: e.drain())

        def v3(t, a, bdim):
            return t[:, :].rearrange("p (a b) -> p a b", b=bdim)

        phase_es = [None]

        def phase_begin():
            if phase_es[0] is not None:
                phase_es[0].close()
            phase_es[0] = ExitStack()
            w8("sync", "sSC")
            w8("sync", "sVE")
            w8("sync", "sPE")
            w8("sync", "sST")

        def psbuf(name, shape, dt=F32):
            return phase_es[0].enter_context(nc.sbuf_tensor(name, shape, dt))

        def _emit():
            def ld_pm(dst, src, ncols):
                return dma("sync", "sLD", dst[:, 0:ncols],
                           bass.AP(src, 0, [[1, 128], [128, ncols]]), slow=True)

            ld_pm(c_bin, b_in, OC)
            ld_pm(c_convb, conv_b, DT)
            ld_pm(c_nbdt, b_dt, DT)
            ld_pm(c_D, Dvec, DT)
            ld_pm(c_bout, b_out, DT)
            ld_pm(c_lng, ln_g, DT)
            ld_pm(c_lnb, ln_b, DT)
            ld_pm(c_b1, b1, HC)
            ld_pm(c_b2, b2, DT)
            dma("sync", "sLD", c_bbc[:, 0:1], bass.AP(b_bc, 0, [[1, 32], [1, 1]]), slow=True)
            dma("sync", "sLD", v3(c_convw, DT, 4),
                bass.AP(conv_w, 0, [[4, 128], [512, DT], [1, 4]]))
            dma("sync", "sLD", c_negA[:, :], A_log_sl[:, :])
            dma("sync", "sLD", c_id[:, :], ident[:, :])
            dma("sync", "sLD", c_onesr[:, :], ones_r[:, :])
            dma("sync", "sLD", c_sel[:, :], selBC[:, :])
            vld = dma("sync", "sLD", c_lead[:, :], leadmask[:, :])
            w8("scalar", "sLD", vld)
            op("scalar", lambda e: e.activation(out=c_negA[:, :], in_=c_negA[:, :], func=AF.Exp))
            tag("scalar", "sSC")
            w8("vector", "sLD", vld)
            op("vector", lambda e: e.tensor_scalar_mul(c_nbdt[:, :], c_nbdt[:, :], -1.0))
            op("vector", lambda e: e.memset(a_amax[:, 0:4], 0.0))
            op("vector", lambda e: e.memset(c_ones128[:, :], 1.0))
            op("vector", lambda e: e.memset(s_state[:, :], 0.0))
            tag("vector", "sVE")
            if upto == "p0":
                raise _StopBuild

            phase_begin()
            q_w = psbuf("q_w", [128, 32 * 1024])
            q_asm = psbuf("q_asm", [128, H], BF16)
            q_i32 = psbuf("q_i32", [128, 128], I32)

            def prep_w(which):
                wsrc, rows, cols = (w1, H, D) if which == 0 else (w2, D, H)
                rchunks = rows // 128
                scol = 4 + which
                wscol = 6 + which
                dsthbm = w1qT_hbm if which == 0 else w2qT_hbm
                nout = cols // 128
                vw = dma("sync", "sLD", q_w[:, 0:rchunks * cols],
                         bass.AP(wsrc, 0, [[cols, 128], [128 * cols, rchunks], [1, cols]]))
                w8("vector", "sLD", vw)
                op("vector", lambda e, rchunks=rchunks, cols=cols: e.tensor_reduce(
                    a_amax[:, 0:1], q_w[:, 0:rchunks * cols], AX.X, ALU.max,
                    apply_absolute_value=True))
                vve = tag("vector", "sVE")
                w8("sync", "sVE", vve)
                vtr = dma("sync", "sWT", g_tr[0:1, 0:128], a_amax[:, 0:1])
                w8("tensor", "sWT", vtr)
                op("tensor", lambda e: e.matmul(pb[6][:, 0:128], c_onesr[0:1, :], g_tr[0:1, :],
                                                start=True, stop=True))
                vpe = tag("tensor", "sPE")
                w8("vector", "sPE", vpe)
                op("vector", lambda e, wscol=wscol: e.tensor_reduce(
                    s_sc[:, wscol:wscol + 1], pb[6][:, 0:128], AX.X, ALU.max))
                dr("vector")
                op("vector", lambda e, wscol=wscol: e.tensor_scalar_max(
                    s_sc[:, wscol:wscol + 1], s_sc[:, wscol:wscol + 1], MIN_RANGE))
                dr("vector")
                op("vector", lambda e, scol=scol, wscol=wscol: e.reciprocal(
                    s_sc[:, scol:scol + 1], s_sc[:, wscol:wscol + 1]))
                dr("vector")
                op("vector", lambda e, scol=scol: e.tensor_scalar_mul(
                    s_sc[:, scol:scol + 1], s_sc[:, scol:scol + 1], QP))
                op("vector", lambda e, wscol=wscol: e.tensor_scalar_mul(
                    s_sc[:, wscol:wscol + 1], s_sc[:, wscol:wscol + 1], 1.0 / QP))
                dr("vector")
                vve0 = tag("vector", "sVE")
                if upto == "prep_amax":
                    raise _StopBuild
                evac_done = {}
                for ocq in range(nout):
                    for rc in range(rchunks):
                        bank = pb[rc % 2]
                        w8("tensor", "sVE", evac_done.get(rc % 2, vve0))
                        op("tensor", lambda e, rc=rc, ocq=ocq, bank=bank, cols=cols: e.transpose(
                            bank[:, 0:128],
                            q_w[:, rc * cols + ocq * 128: rc * cols + ocq * 128 + 128],
                            c_id[:, :]))
                        vt = tag("tensor", "sPE")
                        w8("vector", "sPE", vt)
                        op("vector", lambda e, bank=bank, scol=scol: e.tensor_scalar_mul(
                            q_i32[:, :], bank[:, 0:128], s_sc[:, scol:scol + 1]))
                        dr("vector")
                        op("vector", lambda e, rc=rc: e.tensor_copy(
                            out=q_asm[:, rc * 128: rc * 128 + 128], in_=q_i32[:, :]))
                        evac_done[rc % 2] = tag("vector", "sVE")
                    w8("sync", "sVE", evac_done[(rchunks - 1) % 2])
                    vst = dma("sync", "sST", dsthbm[ocq, :, 0:rows], q_asm[:, 0:rows])
                    w8("vector", "sST", vst)
                    if upto == "prep_q1":
                        raise _StopBuild

            prep_w(0)
            w8("sync", "sVE")
            prep_w(1)
            if upto == "prep":
                raise _StopBuild

            phase_begin()
            a_xtm = psbuf("a_xtm", [128, 4 * D])
            a_xntm = psbuf("a_xntm", [128, 4 * D])
            a_xnfm = psbuf("a_xnfm", [128, DT * TC])
            a_xpad = psbuf("a_xpad", [128, DT * (TC + 4)])
            a_carry = psbuf("a_carry", [128, DT * 4])
            a_out1 = psbuf("a_out1", [128, DT * TC])
            a_xconv = psbuf("a_xconv", [128, DT * TC])
            w_a = psbuf("w_a", [128, DT * DH])

            vWa = dma("sync", "sWB", w_a[:, :],
                      bass.AP(W_in, 0, [[DH, 128], [128 * DH, DT], [1, DH]]))

            dbg_xn = dbg("xn", [DT, 128, TCORE])
            dbg_xconv = dbg("xconv", [CH, DT, 128, TC])
            dbg_dtneg = dbg("dtneg", [CH, DT, 128, TC])
            dbg_bc = dbg("bc", [CH, 32, TC])
            dbg_gate = dbg("gate", [DT, 128, TCORE])
            dbg_mam = dbg("mam", [DT, 128, TCORE])
            dbg_xntm = dbg("xntm", [CH, 128, 4 * D])
            dbg_stat = dbg("stat", [CH, 128, 32])

            def ln_chunk(nt, xoff):
                w8("sync", "sSC")
                v = dma("sync", "sLD", v3(a_xtm, 4, D)[:, 0:nt, :],
                        bass.AP(xx, xoff * D, [[D, 128], [128 * D, nt], [1, D]]))
                w8("vector", "sLD", v)
                op("vector", lambda e, nt=nt: e.tensor_reduce(
                    a_stat[:, 0:nt], v3(a_xtm, 4, D)[:, 0:nt, :], AX.X, ALU.add))
                tag("vector", "sVE")
                w8("scalar", "sLD", v)
                for tt in range(nt):
                    op("scalar", lambda e, tt=tt: e.activation(
                        out=v3(a_xntm, 4, D)[:, tt, :], in_=v3(a_xtm, 4, D)[:, tt, :],
                        func=AF.Square, accum_out=a_stat[:, 8 + tt:9 + tt]))
                vsc = tag("scalar", "sSC")
                w8("vector", "sSC", vsc)
                op("vector", lambda e, nt=nt: e.tensor_scalar_mul(a_stat[:, 16:16 + nt], a_stat[:, 0:nt], 1.0 / D))
                dr("vector")
                op("vector", lambda e, nt=nt: e.tensor_mul(a_stat[:, 24:24 + nt], a_stat[:, 16:16 + nt], a_stat[:, 16:16 + nt]))
                op("vector", lambda e, nt=nt: e.tensor_scalar_mul(a_stat[:, 8:8 + nt], a_stat[:, 8:8 + nt], 1.0 / D))
                dr("vector")
                op("vector", lambda e, nt=nt: e.tensor_sub(a_stat[:, 8:8 + nt], a_stat[:, 8:8 + nt], a_stat[:, 24:24 + nt]))
                dr("vector")
                op("vector", lambda e, nt=nt: e.tensor_scalar_add(a_stat[:, 8:8 + nt], a_stat[:, 8:8 + nt], LN_EPS))
                vve = tag("vector", "sVE")
                w8("scalar", "sVE", vve)
                op("scalar", lambda e, nt=nt: e.activation(
                    out=a_stat[:, 28:28 + nt], in_=a_stat[:, 8:8 + nt], func=AF.Sqrt))
                vsc = tag("scalar", "sSC")
                w8("vector", "sSC", vsc)
                op("vector", lambda e, nt=nt: e.reciprocal(a_stat[:, 24:24 + nt], a_stat[:, 28:28 + nt]))
                dr("vector")
                op("vector", lambda e, nt=nt: e.tensor_mul(a_stat[:, 16:16 + nt], a_stat[:, 16:16 + nt], a_stat[:, 24:24 + nt]))
                dr("vector")
                op("vector", lambda e, nt=nt: e.tensor_scalar_mul(a_stat[:, 16:16 + nt], a_stat[:, 16:16 + nt], -1.0))
                vve = tag("vector", "sVE")
                w8("scalar", "sVE", vve)
                for tt in range(nt):
                    op("scalar", lambda e, tt=tt: e.activation(
                        out=v3(a_xntm, 4, D)[:, tt, :], in_=v3(a_xtm, 4, D)[:, tt, :], func=AF.Identity,
                        scale=a_stat[:, 24 + tt:25 + tt], bias=a_stat[:, 16 + tt:17 + tt]))
                return tag("scalar", "sSC")

            vsc = ln_chunk(1, 0)
            w8("tensor", "sSC", vsc)
            w8("tensor", "sWB", vWa)
            for dt_ in range(DT):
                op("tensor", lambda e, dt_=dt_: e.transpose(
                    pb[4][:, dt_ * 4: dt_ * 4 + 3],
                    v3(a_xntm, 4, D)[0:3, 0, dt_ * 128:(dt_ + 1) * 128], c_id[0:3, 0:3]))
            vpe = tag("tensor", "sPE")
            w8("vector", "sPE", vpe)
            op("vector", lambda e: e.tensor_copy(out=a_carry[:, 0:DT * 4], in_=pb[4][:, 0:DT * 4]))
            vve = tag("vector", "sVE")
            w8("tensor", "sVE", vve)
            for oc_ in range(DT):
                for kt in range(DT):
                    op("tensor", lambda e, oc_=oc_, kt=kt: e.matmul(
                        pb[5][:, oc_ * 4: oc_ * 4 + 3],
                        v3(w_a, DT, DH)[:, kt, oc_ * 128:(oc_ + 1) * 128],
                        a_carry[:, kt * 4:kt * 4 + 3],
                        start=(kt == 0 and oc_ == 0), stop=(kt == DT - 1 and oc_ == DT - 1)))
            vpe = tag("tensor", "sPE")
            w8("scalar", "sPE", vpe)
            for oc_ in range(DT):
                op("scalar", lambda e, oc_=oc_: e.activation(
                    out=v3(a_xpad, DT, TC + 4)[:, oc_, 0:3], in_=pb[5][:, oc_ * 4:oc_ * 4 + 3],
                    func=AF.Identity, bias=c_bin[:, oc_:oc_ + 1]))
            vsc_lead = tag("scalar", "sSC")
            w8("vector", "sSC", vsc_lead)
            for dt_ in range(DT):
                op("vector", lambda e, dt_=dt_: e.tensor_scalar_mul(
                    v3(a_carry, DT, 4)[:, dt_, 0:3], v3(a_xpad, DT, TC + 4)[:, dt_, 0:3],
                    c_lead[:, 0:1]))
            tag("vector", "sVE")
            evac = {0: ("sVE", ctr["sVE"]), 1: ("sSC", vsc_lead)}
            if upto == "lead":
                raise _StopBuild

            for c in range(CH):
                st_prev = ctr["sST"]
                vsc = ln_chunk(4, 3 + c * TC)
                if dbg_xntm is not None:
                    w8("sync", "sSC", vsc)
                    dma("sync", "sST", bass.AP(dbg_xntm, c * 128 * 4 * D, [[4 * D, 128], [1, 4 * D]]),
                        a_xntm[:, :])
                    dma("sync", "sST", bass.AP(dbg_stat, c * 128 * 32, [[32, 128], [1, 32]]),
                        a_stat[:, :])
                    dbg_quiesce()
                w8("vector", "sSC", vsc)
                op("vector", lambda e: e.tensor_reduce(
                    a_stat[:, 0:4], v3(a_xntm, 4, D)[:, :, :], AX.X, ALU.max, apply_absolute_value=True))
                dr("vector")
                op("vector", lambda e: e.tensor_reduce(
                    a_stat[:, 4:5], a_stat[:, 0:4], AX.X, ALU.max))
                dr("vector")
                op("vector", lambda e: e.tensor_max(a_amax[:, 2:3], a_amax[:, 2:3], a_stat[:, 4:5]))
                tag("vector", "sVE")
                w8("vector", "sST", st_prev)
                w8("tensor", "sSC", vsc)
                for g in range(8):
                    bank = pb[4 + (g % 2)]
                    w8("tensor", *evac[g % 2])
                    for j in range(4):
                        tt, dt_ = divmod(g * 4 + j, DT)
                        op("tensor", lambda e, bank=bank, j=j, tt=tt, dt_=dt_: e.transpose(
                            bank[:, j * 128:(j + 1) * 128],
                            v3(a_xntm, 4, D)[:, tt, dt_ * 128:(dt_ + 1) * 128], c_id[:, :]))
                    vpe = tag("tensor", "sPE")
                    w8("vector", "sPE", vpe)
                    for j in range(4):
                        tt, dt_ = divmod(g * 4 + j, DT)
                        op("vector", lambda e, bank=bank, j=j, tt=tt, dt_=dt_: e.tensor_copy(
                            out=v3(a_xnfm, DT, TC)[:, dt_, tt * 128:(tt + 1) * 128],
                            in_=bank[:, j * 128:(j + 1) * 128]))
                    evac[g % 2] = ("sVE", tag("vector", "sVE"))
                vve_xn = ctr["sVE"]
                w8("sync", "sVE", vve_xn)
                dma("sync", "sST", bass.AP(xn_hbm, c * TC, [[TCORE, 128], [128 * TCORE, DT], [1, TC]]),
                    v3(a_xnfm, DT, TC)[:, :, :])
                if dbg_xn is not None:
                    dma("sync", "sST", bass.AP(dbg_xn, c * TC, [[TCORE, 128], [128 * TCORE, DT], [1, TC]]),
                        v3(a_xnfm, DT, TC)[:, :, :])

                w8("tensor", "sVE", vve_xn)
                evp = {}
                for oc_ in range(OC):
                    bank = pb[oc_ % 2]
                    if oc_ >= 2:
                        w8("tensor", "sSC", evp[oc_ % 2])
                    for kt in range(DT):
                        op("tensor", lambda e, oc_=oc_, kt=kt, bank=bank: e.matmul(
                            bank[:, :], v3(w_a, DT, DH)[:, kt, oc_ * 128:(oc_ + 1) * 128],
                            v3(a_xnfm, DT, TC)[:, kt, :], start=(kt == 0), stop=(kt == DT - 1)))
                    vpe = tag("tensor", "sPE")
                    w8("scalar", "sPE", vpe)
                    if oc_ == 0:
                        w8("scalar", "sST", st_prev)
                    if oc_ < DT:
                        op("scalar", lambda e, oc_=oc_, bank=bank: e.activation(
                            out=v3(a_xpad, DT, TC + 4)[:, oc_, 3:3 + TC], in_=bank[:, :],
                            func=AF.Identity, bias=c_bin[:, oc_:oc_ + 1]))
                    else:
                        op("scalar", lambda e, oc_=oc_, bank=bank: e.activation(
                            out=v3(a_out1, DT, TC)[:, oc_ - DT, :], in_=bank[:, :],
                            func=AF.Silu, bias=c_bin[:, oc_:oc_ + 1]))
                    evp[oc_ % 2] = tag("scalar", "sSC")
                vsc_proj = ctr["sSC"]
                w8("sync", "sSC", vsc_proj)
                dma("sync", "sST", bass.AP(gate_hbm, c * TC, [[TCORE, 128], [128 * TCORE, DT], [1, TC]]),
                    v3(a_out1, DT, TC)[:, :, :])
                if dbg_gate is not None:
                    dma("sync", "sST", bass.AP(dbg_gate, c * TC, [[TCORE, 128], [128 * TCORE, DT], [1, TC]]),
                        v3(a_out1, DT, TC)[:, :, :])

                w8("vector", "sSC", vsc_proj)
                w8("vector", "sST", st_prev)
                for dt_ in range(DT):
                    op("vector", lambda e, dt_=dt_: e.tensor_copy(
                        out=v3(a_xpad, DT, TC + 4)[:, dt_, 0:3], in_=v3(a_carry, DT, 4)[:, dt_, 0:3]))
                for dt_ in range(DT):
                    op("vector", lambda e, dt_=dt_: e.tensor_scalar_mul(
                        v3(a_xconv, DT, TC)[:, dt_, :], v3(a_xpad, DT, TC + 4)[:, dt_, 0:TC],
                        v3(c_convw, DT, 4)[:, dt_, 0:1]))
                    for k in range(1, 4):
                        op("vector", lambda e, dt_=dt_, k=k: e.scalar_tensor_tensor(
                            out=v3(a_xconv, DT, TC)[:, dt_, :],
                            in0=v3(a_xpad, DT, TC + 4)[:, dt_, k:k + TC],
                            scalar=v3(c_convw, DT, 4)[:, dt_, k:k + 1],
                            in1=v3(a_xconv, DT, TC)[:, dt_, :],
                            op0=ALU.mult, op1=ALU.add))
                    op("vector", lambda e, dt_=dt_: e.tensor_copy(
                        out=v3(a_carry, DT, 4)[:, dt_, 0:3],
                        in_=v3(a_xpad, DT, TC + 4)[:, dt_, TC:3 + TC]))
                vve = tag("vector", "sVE")
                w8("scalar", "sVE", vve)
                for dt_ in range(DT):
                    op("scalar", lambda e, dt_=dt_: e.activation(
                        out=v3(a_xconv, DT, TC)[:, dt_, :], in_=v3(a_xconv, DT, TC)[:, dt_, :],
                        func=AF.Silu, bias=c_convb[:, dt_:dt_ + 1]))
                vsc_conv = tag("scalar", "sSC")
                w8("sync", "sSC", vsc_conv)
                dma("sync", "sST", bass.AP(a2a_in[c], 128 * TC, [[TC, 128], [2 * 128 * TC, DT], [1, TC]]),
                    v3(a_xconv, DT, TC)[:, :, :])
                if dbg_xconv is not None:
                    dma("sync", "sST", bass.AP(dbg_xconv, c * DT * 128 * TC, [[TC, 128], [128 * TC, DT], [1, TC]]),
                        v3(a_xconv, DT, TC)[:, :, :])
            if upto == "a1":
                raise _StopBuild

            def global_scale(amax_col, arin, arout, rcol, sccol):
                w8("sync", "sVE")
                dma("sync", "sST", arin[:, :], amax_col)
                w8("gpsimd", "sST")
                ctr["sCC"] += 1
                op("gpsimd", lambda e, arin=arin, arout=arout: e.collective_compute(
                    "AllReduce", ALU.max, ins=[arin[:, :]], outs=[arout[:, :]],
                    replica_groups=[list(range(8))]).then_inc(sems["sCC"], 1))
                w8("sync", "sCC")
                vldx = dma("sync", "sWT", g_tr[0:1, 0:128], arout[:, :])
                w8("tensor", "sWT", vldx)
                op("tensor", lambda e: e.matmul(pb[6][:, 0:128], c_onesr[0:1, :], g_tr[0:1, :],
                                                start=True, stop=True))
                vpe = tag("tensor", "sPE")
                w8("vector", "sPE", vpe)
                op("vector", lambda e, sccol=sccol: e.tensor_reduce(
                    s_sc[:, sccol:sccol + 1], pb[6][:, 0:128], AX.X, ALU.max))
                dr("vector")
                op("vector", lambda e, sccol=sccol: e.tensor_scalar_max(
                    s_sc[:, sccol:sccol + 1], s_sc[:, sccol:sccol + 1], MIN_RANGE))
                dr("vector")
                op("vector", lambda e, rcol=rcol, sccol=sccol: e.reciprocal(
                    s_sc[:, rcol:rcol + 1], s_sc[:, sccol:sccol + 1]))
                dr("vector")
                op("vector", lambda e, rcol=rcol: e.tensor_scalar_mul(
                    s_sc[:, rcol:rcol + 1], s_sc[:, rcol:rcol + 1], QP))
                op("vector", lambda e, sccol=sccol: e.tensor_scalar_mul(
                    s_sc[:, sccol:sccol + 1], s_sc[:, sccol:sccol + 1], 1.0 / QP))
                dr("vector")
                tag("vector", "sVE")

            global_scale(a_amax[:, 2:3], ar1_in, ar1_out, 0, 3)
            op("vector", lambda e: e.tensor_mul(s_sc[:, 1:2], s_sc[:, 6:7], s_sc[:, 3:4]))
            dr("vector")
            op("vector", lambda e: e.tensor_scalar_mul(s_b1s[:, :], c_b1[:, :], s_sc[:, 1:2]))
            tag("vector", "sVE")
            if upto == "ar1":
                raise _StopBuild

            phase_begin()
            a_xc2 = psbuf("a_xc2", [128, DT * TC])
            a_out2 = psbuf("a_out2", [128, DT * TC])
            a_bc = psbuf("a_bc", [32, TC])
            w_b = psbuf("w_b", [128, DT * D])
            w_c = psbuf("w_c", [128, DT * 2 * N])

            dma("sync", "sWB", w_b[:, :],
                bass.AP(W_dt, 0, [[D, 128], [128 * D, DT], [1, D]]))
            vWb = dma("sync", "sWB", w_c[:, :],
                      bass.AP(W_bc, 0, [[2 * N, 128], [128 * 2 * N, DT], [1, 2 * N]]))

            for c in range(CH):
                w8("sync", "sSC")
                w8("sync", "sVE")
                w8("sync", "sST")
                vxc = dma("sync", "sLD", v3(a_xc2, DT, TC)[:, :, :],
                          bass.AP(a2a_in[c], 128 * TC, [[TC, 128], [2 * 128 * TC, DT], [1, TC]]))
                w8("tensor", "sLD", vxc)
                w8("tensor", "sWB", vWb)
                evd = {}
                for oc_ in range(DT):
                    bank = pb[oc_ % 2]
                    if oc_ >= 2:
                        w8("tensor", "sSC", evd[oc_ % 2])
                    for kt in range(DT):
                        op("tensor", lambda e, oc_=oc_, kt=kt, bank=bank: e.matmul(
                            bank[:, :], v3(w_b, DT, D)[:, kt, oc_ * 128:(oc_ + 1) * 128],
                            v3(a_xc2, DT, TC)[:, kt, :], start=(kt == 0), stop=(kt == DT - 1)))
                    vpe = tag("tensor", "sPE")
                    w8("scalar", "sPE", vpe)
                    op("scalar", lambda e, oc_=oc_, bank=bank: e.activation(
                        out=v3(a_out2, DT, TC)[:, oc_, :], in_=bank[:, :],
                        func=AF.Sigmoid, scale=-1.0, bias=c_nbdt[:, oc_:oc_ + 1]))
                    op("scalar", lambda e, oc_=oc_: e.activation(
                        out=v3(a_out2, DT, TC)[:, oc_, :], in_=v3(a_out2, DT, TC)[:, oc_, :],
                        func=AF.Ln))
                    evd[oc_ % 2] = tag("scalar", "sSC")
                bank = pb[2]
                for kt in range(DT):
                    op("tensor", lambda e, kt=kt, bank=bank: e.matmul(
                        bank[0:32, :], v3(w_c, DT, 2 * N)[:, kt, :],
                        v3(a_xc2, DT, TC)[:, kt, :], start=(kt == 0), stop=(kt == DT - 1)))
                vpe_bc = tag("tensor", "sPE")
                w8("scalar", "sPE", vpe_bc)
                op("scalar", lambda e, bank=bank: e.activation(
                    out=a_bc[:, :], in_=bank[0:32, :], func=AF.Identity, bias=c_bbc[:, 0:1]))
                vsc_all = tag("scalar", "sSC")
                w8("vector", "sSC", vsc_all)
                op("vector", lambda e: e.tensor_scalar_max(a_out2[:, :], a_out2[:, :], -80.0))
                vve_dt = tag("vector", "sVE")
                w8("sync", "sVE", vve_dt)
                dma("sync", "sST", bass.AP(a2a_in[c], 0, [[TC, 128], [2 * 128 * TC, DT], [1, TC]]),
                    v3(a_out2, DT, TC)[:, :, :])
                dma("sync", "sST", bc_in[c][:, :], a_bc[:, :])
                if dbg_dtneg is not None:
                    dma("sync", "sST", bass.AP(dbg_dtneg, c * DT * 128 * TC, [[TC, 128], [128 * TC, DT], [1, TC]]),
                        v3(a_out2, DT, TC)[:, :, :])
                if dbg_bc is not None:
                    dma("sync", "sST", bass.AP(dbg_bc, c * 32 * TC, [[TC, 32], [1, TC]]), a_bc[:, :])
                vst = ctr["sST"]
                w8("gpsimd", "sST", vst)
                ctr["sCC"] += 1
                op("gpsimd", lambda e, c=c: e.collective_compute(
                    "AllToAll", ALU.bypass, ins=[a2a_in[c][:, :, :, :]],
                    outs=[a2a_out[c][:, :, :, :]], replica_groups=[list(range(8))]).then_inc(sems["sCC"], 1))
                ctr["sCC"] += 1
                op("gpsimd", lambda e, c=c: e.collective_compute(
                    "AllGather", ALU.bypass, ins=[bc_in[c][:, :]],
                    outs=[bc_out[c][:, :, :]], replica_groups=[list(range(8))]).then_inc(sems["sCC"], 1))
            vcc_a2 = ctr["sCC"]
            if upto == "a2":
                raise _StopBuild

            phase_begin()
            f_xn = psbuf("f_xn", [128, DT * TC])
            f_xq = psbuf("f_xq", [128, DT * TC], BF16)
            f_i32 = psbuf("f_i32", [128, TC], I32)
            f_h = psbuf("f_h", [128, 8 * TC])
            w_q1 = psbuf("w_q1", [128, DT * H], BF16)

            vwq = dma("sync", "sWB", w_q1[:, :],
                      bass.AP(w1qT_hbm, 0, [[H, 128], [128 * H, DT], [1, H]]))
            dbg_hff = dbg("hff", [HC, 128, TCORE])
            dbg_sc = dbg("sc", [128, 8])
            vpe_ffn_prev = 0
            vve_q_prev = 0
            for c in range(CH):
                stp = ctr["sST"]
                w8("sync", "sVE", vve_q_prev)
                vx = dma("sync", "sLD", v3(f_xn, DT, TC)[:, :, :],
                         bass.AP(xn_hbm, c * TC, [[TCORE, 128], [128 * TCORE, DT], [1, TC]]))
                w8("vector", "sLD", vx)
                w8("vector", "sPE", vpe_ffn_prev)
                for dt_ in range(DT):
                    op("vector", lambda e, dt_=dt_: e.tensor_scalar_mul(
                        f_i32[:, :], v3(f_xn, DT, TC)[:, dt_, :], s_sc[:, 0:1]))
                    op("vector", lambda e, dt_=dt_: e.tensor_copy(
                        out=v3(f_xq, DT, TC)[:, dt_, :], in_=f_i32[:, :]))
                vve_q = tag("vector", "sVE")
                vve_q_prev = vve_q
                w8("tensor", "sVE", vve_q)
                w8("tensor", "sWB", vwq)
                evf = {}
                for hc_ in range(HC):
                    bank = pb[hc_ % 2]
                    if hc_ >= 2:
                        w8("tensor", "sSC", evf[hc_ % 2])
                    for kt in range(DT):
                        op("tensor", lambda e, hc_=hc_, kt=kt, bank=bank: e.matmul(
                            bank[:, :], v3(w_q1, DT, H)[:, kt, hc_ * 128:(hc_ + 1) * 128],
                            v3(f_xq, DT, TC)[:, kt, :], start=(kt == 0), stop=(kt == DT - 1)))
                    vpe = tag("tensor", "sPE")
                    w8("scalar", "sPE", vpe)
                    if hc_ % 8 == 0:
                        w8("scalar", "sST", stp if hc_ == 0 else ctr["sST"])
                    op("scalar", lambda e, hc_=hc_, bank=bank: e.activation(
                        out=v3(f_h, 8, TC)[:, hc_ % 8, :], in_=bank[:, :],
                        func=AF.Gelu, scale=s_sc[:, 1:2], bias=s_b1s[:, hc_:hc_ + 1]))
                    evf[hc_ % 2] = tag("scalar", "sSC")
                    if hc_ % 8 == 7:
                        vsc_h = ctr["sSC"]
                        w8("vector", "sSC", vsc_h)
                        op("vector", lambda e: e.tensor_reduce(
                            a_stat[:, 0:8], v3(f_h, 8, TC)[:, :, :], AX.X, ALU.max,
                            apply_absolute_value=True))
                        dr("vector")
                        op("vector", lambda e: e.tensor_reduce(
                            a_stat[:, 8:9], a_stat[:, 0:8], AX.X, ALU.max))
                        dr("vector")
                        op("vector", lambda e: e.tensor_max(a_amax[:, 1:2], a_amax[:, 1:2], a_stat[:, 8:9]))
                        tag("vector", "sVE")
                        w8("sync", "sSC", vsc_h)
                        dma("sync", "sST",
                            bass.AP(h_hbm, (hc_ - 7) * 128 * TCORE + c * TC,
                                    [[TCORE, 128], [128 * TCORE, 8], [1, TC]]),
                            v3(f_h, 8, TC)[:, :, :])
                        if dbg_hff is not None:
                            dma("sync", "sST",
                                bass.AP(dbg_hff, (hc_ - 7) * 128 * TCORE + c * TC,
                                        [[TCORE, 128], [128 * TCORE, 8], [1, TC]]),
                                v3(f_h, 8, TC)[:, :, :])
                            dbg_quiesce()
                        vpe_ffn_prev = ctr["sPE"]

            global_scale(a_amax[:, 1:2], ar2_in, ar2_out, 2, 3)
            op("vector", lambda e: e.tensor_mul(s_sc[:, 3:4], s_sc[:, 7:8], s_sc[:, 3:4]))
            dr("vector")
            op("vector", lambda e: e.tensor_scalar_mul(s_b2s[:, :], c_b2[:, :], s_sc[:, 3:4]))
            tag("vector", "sVE")
            if dbg_sc is not None:
                w8("sync", "sVE")
                dma("sync", "sST", dbg_sc[:, :], s_sc[:, :])
            if upto == "ffn1":
                raise _StopBuild

            phase_begin()
            s_dt = psbuf("s_dt", [128, 2 * TC])
            s_xc = psbuf("s_xc", [128, 2 * TC])
            s_dtx = psbuf("s_dtx", [128, 2 * TC])
            s_bc2 = psbuf("s_bc2", [64, 2 * TC])
            s_bchl = psbuf("s_bchl", [64, 2 * TC], BF16)
            s_dA = psbuf("s_dA", [128, 2 * TC])
            s_u = psbuf("s_u", [128, 2 * TC])
            s_hint = psbuf("s_hint", [128, N * TC])
            s_prod = psbuf("s_prod", [128, N * TC])
            s_y = psbuf("s_y", [128, TC])

            seg_done = {}
            segi = 0
            for hh in range(2):
                for c2 in range(CH):
                    for b_ in range(4):
                        slot = 2 * b_ + hh
                        sp_ = segi % 2
                        o_ = sp_ * TC
                        w8("sync", "sCC", min(2 * (c2 + 1), vcc_a2))
                        ch_ = "sLD" if sp_ == 0 else "sLE"
                        if sp_ in seg_done:
                            w8("sync", "sSC", seg_done[sp_][0])
                            w8("sync", "sVE", seg_done[sp_][1])
                        dma("sync", ch_, s_dt[:, o_:o_ + TC], a2a_out[c2][slot, 0, :, :])
                        dma("sync", ch_, s_xc[:, o_:o_ + TC], a2a_out[c2][slot, 1, :, :])
                        dma("sync", ch_, s_bc2[0:32, o_:o_ + TC], bc_out[c2][slot, :, :])
                        vbc = dma("sync", ch_, s_bc2[32:64, o_:o_ + TC], bc_out[c2][slot, :, :])
                        w8("vector", ch_, vbc)
                        op("vector", lambda e, o_=o_: e.tensor_mul(
                            s_dtx[:, o_:o_ + TC], s_dt[:, o_:o_ + TC], s_xc[:, o_:o_ + TC]))
                        op("vector", lambda e, o_=o_: e.tensor_copy(
                            out=s_bchl[0:32, o_:o_ + TC], in_=s_bc2[0:32, o_:o_ + TC]))
                        op("vector", lambda e, o_=o_: e.tensor_copy(
                            out=s_bchl[32:64, o_:o_ + TC], in_=s_bc2[32:64, o_:o_ + TC]))
                        op("vector", lambda e, o_=o_: e.tensor_sub(
                            s_bchl[32:64, o_:o_ + TC], s_bc2[32:64, o_:o_ + TC], s_bchl[32:64, o_:o_ + TC]))
                        vve_pre = tag("vector", "sVE")
                        vu = {}
                        vscan = {}
                        vprod = {}
                        for n_ in range(N):
                            par = n_ % 2
                            pbB = pb[4 + par]
                            pbC = pb[6 + par]
                            w8("tensor", "sVE", vu.get(par, vve_pre))
                            op("tensor", lambda e, n_=n_, pbB=pbB, o_=o_: e.matmul(
                                pbB[:, :], c_sel[:, n_ * 128:(n_ + 1) * 128], s_bchl[:, o_:o_ + TC],
                                start=True, stop=True))
                            w8("tensor", "sVE", vprod.get(par, vve_pre))
                            op("tensor", lambda e, n_=n_, pbC=pbC, o_=o_: e.matmul(
                                pbC[:, :], c_sel[:, (16 + n_) * 128:(17 + n_) * 128], s_bchl[:, o_:o_ + TC],
                                start=True, stop=True))
                            vpe_n = tag("tensor", "sPE")
                            w8("scalar", "sVE", vscan.get(par, vve_pre))
                            op("scalar", lambda e, n_=n_, par=par, o_=o_: e.activation(
                                out=s_dA[:, par * TC:(par + 1) * TC], in_=s_dt[:, o_:o_ + TC],
                                func=AF.Exp, scale=c_negA[:, n_:n_ + 1]))
                            vscal = tag("scalar", "sSC")
                            w8("vector", "sPE", vpe_n)
                            op("vector", lambda e, par=par, pbB=pbB, o_=o_: e.tensor_mul(
                                s_u[:, par * TC:(par + 1) * TC], s_dtx[:, o_:o_ + TC], pbB[:, :]))
                            vu[par] = tag("vector", "sVE")
                            w8("vector", "sSC", vscal)
                            op("vector", lambda e, n_=n_, b_=b_, par=par: e.tensor_tensor_scan(
                                out=bass.AP(s_hint, n_, [[N * TC, 128], [N, TC]]),
                                data0=s_dA[:, par * TC:(par + 1) * TC],
                                data1=s_u[:, par * TC:(par + 1) * TC],
                                initial=s_state[:, b_ * N + n_: b_ * N + n_ + 1],
                                op0=ALU.mult, op1=ALU.add))
                            vscan[par] = tag("vector", "sVE")
                            op("vector", lambda e, n_=n_, pbC=pbC: e.tensor_mul(
                                v3(s_prod, TC, N)[:, :, n_:n_ + 1],
                                v3(s_hint, TC, N)[:, :, n_:n_ + 1],
                                pbC[:, :]))
                            vprod[par] = tag("vector", "sVE")
                        op("vector", lambda e, b_=b_: e.tensor_copy(
                            out=s_state[:, b_ * N:(b_ + 1) * N],
                            in_=s_hint[:, (TC - 1) * N: TC * N]))
                        w8("vector", "sST")
                        op("vector", lambda e: e.tensor_reduce(
                            s_y[:, :], v3(s_prod, TC, N)[:, :, :], AX.X, ALU.add))
                        vve_y = tag("vector", "sVE")
                        seg_done[sp_] = (ctr["sSC"], vve_y)
                        segi += 1
                        w8("sync", "sVE", vve_y)
                        dma("sync", "sST", ya_in[c2][slot, :, :], s_y[:, :])
                    if hh == 1:
                        vst = ctr["sST"]
                        w8("gpsimd", "sST", vst)
                        ctr["sCC"] += 1
                        op("gpsimd", lambda e, c2=c2: e.collective_compute(
                            "AllToAll", ALU.bypass, ins=[ya_in[c2][:, :, :]],
                            outs=[ya_out[c2][:, :, :]], replica_groups=[list(range(8))]).then_inc(sems["sCC"], 1))

            vcc_scan = ctr["sCC"]
            if upto == "scan":
                raise _StopBuild

            phase_begin()
            p_yfm = psbuf("p_yfm", [128, 2 * DT * TC])
            p_aux = psbuf("p_aux", [128, DT * TC])
            p_gate = psbuf("p_gate", [128, DT * TC])
            p_xn = psbuf("p_xn", [128, DT * TC])
            p_res = psbuf("p_res", [128, DT * TC])
            p_tmp = psbuf("p_tmp", [128, DT * TC])
            p_mu = psbuf("p_mu", [128, TC])
            p_rs = psbuf("p_rs", [128, TC])
            w_o = psbuf("w_o", [128, DT * D])

            vWo = dma("sync", "sWB", w_o[:, :],
                      bass.AP(W_out, 0, [[D, 128], [128 * D, DT], [1, D]]))
            pv = {}
            for c in range(CH):
                w8("sync", "sCC", vcc_scan - (CH - 1 - c))
                if c == 0:
                    w8("sync", "sSC")
                    w8("sync", "sVE")
                    w8("sync", "sST")
                yf3 = p_yfm[:, (c % 2) * DT * TC:(c % 2 + 1) * DT * TC].rearrange(
                    "p (a b) -> p a b", b=TC)
                w8("sync", "sVE", pv.get("y2", 0))
                dma("sync", "sLD", v3(p_aux, DT, TC)[:, :, :],
                    bass.AP(a2a_in[c], 128 * TC, [[TC, 128], [2 * 128 * TC, DT], [1, TC]]))
                dma("sync", "sLD", v3(p_gate, DT, TC)[:, :, :],
                    bass.AP(gate_hbm, c * TC, [[TCORE, 128], [128 * TCORE, DT], [1, TC]]))
                vld3 = dma("sync", "sLD", yf3[:, :, :],
                           bass.AP(ya_out[c], 0, [[TC, 128], [128 * TC, 8], [1, TC]]))
                w8("sync", "sVE", pv.get("res", 0))
                vxn = dma("sync", "sLE", v3(p_xn, DT, TC)[:, :, :],
                          bass.AP(xn_hbm, c * TC, [[TCORE, 128], [128 * TCORE, DT], [1, TC]]))
                w8("vector", "sLD", vld3)
                for dt_ in range(DT):
                    op("vector", lambda e, dt_=dt_, yf3=yf3: e.scalar_tensor_tensor(
                        out=v3(p_tmp, DT, TC)[:, dt_, :], in0=v3(p_aux, DT, TC)[:, dt_, :],
                        scalar=c_D[:, dt_:dt_ + 1], in1=yf3[:, dt_, :],
                        op0=ALU.mult, op1=ALU.add))
                    op("vector", lambda e, dt_=dt_, yf3=yf3: e.tensor_mul(
                        yf3[:, dt_, :], v3(p_tmp, DT, TC)[:, dt_, :],
                        v3(p_gate, DT, TC)[:, dt_, :]))
                vve_y2 = tag("vector", "sVE")
                pv["y2"] = vve_y2
                w8("tensor", "sVE", vve_y2)
                w8("tensor", "sWB", vWo)
                w8("tensor", "sSC", pv.get("evo1", 0))
                w8("tensor", "sVE", pv.get("st", 0))
                w8("scalar", "sST", pv.get("vst", 0))
                evo = {}
                for oc_ in range(DT):
                    bank = pb[oc_ % 2]
                    if oc_ >= 2:
                        w8("tensor", "sSC", evo[oc_ % 2])
                    for kt in range(DT):
                        op("tensor", lambda e, oc_=oc_, kt=kt, bank=bank, yf3=yf3: e.matmul(
                            bank[:, :], v3(w_o, DT, D)[:, kt, oc_ * 128:(oc_ + 1) * 128],
                            yf3[:, kt, :], start=(kt == 0), stop=(kt == DT - 1)))
                    vpe = tag("tensor", "sPE")
                    w8("scalar", "sPE", vpe)
                    op("scalar", lambda e, oc_=oc_, bank=bank: e.activation(
                        out=v3(p_tmp, DT, TC)[:, oc_, :], in_=bank[:, :],
                        func=AF.Identity, bias=c_bout[:, oc_:oc_ + 1]))
                    evo[oc_ % 2] = tag("scalar", "sSC")
                pv["evo1"] = max(evo.values())
                pv["wo"] = ctr["sPE"]
                vsc_wo = ctr["sSC"]
                w8("vector", "sSC", vsc_wo)
                w8("vector", "sLE", vxn)
                for dt_ in range(DT):
                    op("vector", lambda e, dt_=dt_: e.tensor_add(
                        v3(p_res, DT, TC)[:, dt_, :], v3(p_tmp, DT, TC)[:, dt_, :],
                        v3(p_xn, DT, TC)[:, dt_, :]))
                vve_res = tag("vector", "sVE")
                pv["res"] = vve_res
                w8("scalar", "sVE", vve_res)
                for dt_ in range(DT):
                    op("scalar", lambda e, dt_=dt_: e.activation(
                        out=v3(p_tmp, DT, TC)[:, dt_, :], in_=v3(p_res, DT, TC)[:, dt_, :],
                        func=AF.Square))
                vsc_sq = tag("scalar", "sSC")
                w8("tensor", "sSC", vsc_sq)
                for kt in range(DT):
                    op("tensor", lambda e, kt=kt: e.matmul(
                        pb[2][:, :], c_ones128[:, :], v3(p_res, DT, TC)[:, kt, :],
                        start=(kt == 0), stop=(kt == DT - 1)))
                for kt in range(DT):
                    op("tensor", lambda e, kt=kt: e.matmul(
                        pb[3][:, :], c_ones128[:, :], v3(p_tmp, DT, TC)[:, kt, :],
                        start=(kt == 0), stop=(kt == DT - 1)))
                vpe_s = tag("tensor", "sPE")
                w8("vector", "sPE", vpe_s)
                op("vector", lambda e: e.tensor_scalar_mul(p_mu[:, :], pb[2][:, :], 1.0 / D))
                op("vector", lambda e: e.tensor_scalar_mul(p_rs[:, :], pb[3][:, :], 1.0 / D))
                dr("vector")
                op("vector", lambda e: e.tensor_mul(p_tmp[:, 0:TC], p_mu[:, :], p_mu[:, :]))
                dr("vector")
                op("vector", lambda e: e.tensor_sub(p_rs[:, :], p_rs[:, :], p_tmp[:, 0:TC]))
                dr("vector")
                op("vector", lambda e: e.tensor_scalar_add(p_rs[:, :], p_rs[:, :], LN_EPS))
                vve_st = tag("vector", "sVE")
                pv["st"] = vve_st
                w8("scalar", "sVE", vve_st)
                op("scalar", lambda e: e.activation(
                    out=p_tmp[:, 0:TC], in_=p_rs[:, :], func=AF.Sqrt))
                vsc_st0 = tag("scalar", "sSC")
                w8("vector", "sSC", vsc_st0)
                op("vector", lambda e: e.reciprocal(p_rs[:, :], p_tmp[:, 0:TC]))
                vsc_st = tag("vector", "sVE")
                w8("vector", "sVE", vsc_st)
                for dt_ in range(DT):
                    op("vector", lambda e, dt_=dt_: e.tensor_sub(
                        v3(p_res, DT, TC)[:, dt_, :], v3(p_res, DT, TC)[:, dt_, :], p_mu[:, :]))
                    op("vector", lambda e, dt_=dt_: e.tensor_mul(
                        v3(p_res, DT, TC)[:, dt_, :], v3(p_res, DT, TC)[:, dt_, :], p_rs[:, :]))
                vve_n2 = tag("vector", "sVE")
                w8("scalar", "sVE", vve_n2)
                for dt_ in range(DT):
                    op("scalar", lambda e, dt_=dt_: e.activation(
                        out=v3(p_tmp, DT, TC)[:, dt_, :], in_=v3(p_res, DT, TC)[:, dt_, :],
                        func=AF.Identity, scale=c_lng[:, dt_:dt_ + 1], bias=c_lnb[:, dt_:dt_ + 1]))
                vsc_m = tag("scalar", "sSC")
                w8("sync", "sSC", vsc_m)
                dma("sync", "sST", bass.AP(mamba_hbm, c * TC, [[TCORE, 128], [128 * TCORE, DT], [1, TC]]),
                    v3(p_tmp, DT, TC)[:, :, :])
                pv["vst"] = ctr["sST"]
                if dbg_mam is not None:
                    dma("sync", "sST", bass.AP(dbg_mam, c * TC, [[TCORE, 128], [128 * TCORE, DT], [1, TC]]),
                        v3(p_tmp, DT, TC)[:, :, :])
            if upto == "d1":
                raise _StopBuild

            phase_begin()
            g_mam = psbuf("g_mam", [128, DT * TC])
            g_tmp = psbuf("g_tmp", [128, DT * TC])
            g_ttm = psbuf("g_ttm", [128, 4 * D])
            g_xtm = psbuf("g_xtm", [128, 4 * D])
            g_hin = psbuf("g_hin", [128, 8 * TC])
            g_hq = psbuf("g_hq", [128, 8 * TC], BF16)
            g_i32 = psbuf("g_i32", [128, 4 * TC], I32)
            w_q2 = psbuf("w_q2", [128, HC * D], BF16)

            dbg_ffn = dbg("ffn", [DT, 128, TCORE])
            vwq2 = dma("sync", "sWB", w_q2[:, :],
                       bass.AP(w2qT_hbm, 0, [[D, 128], [128 * D, HC], [1, D]]))
            vhq = {}
            vq = {}
            prev = {}
            NG = HC // 4
            for c in range(CH):
                if c == 0:
                    w8("sync", "sSC")
                    w8("sync", "sVE")
                    w8("sync", "sST")
                for grp in range(2):
                    par = grp % 2
                    ch_ = "sLD" if par == 0 else "sLE"
                    if par in vq:
                        w8("sync", "sVE", vq[par])
                    vh = dma("sync", ch_, g_hin[:, par * 4 * TC:(par + 1) * 4 * TC],
                             bass.AP(h_hbm, grp * 4 * 128 * TCORE + c * TC,
                                     [[TCORE, 128], [128 * TCORE, 4], [1, TC]]))
                    prev[("vh", grp)] = vh
                if c > 0:
                    w8("sync", "sVE", prev["vve_tot"])
                vm = dma("sync", "sWT", v3(g_mam, DT, TC)[:, :, :],
                         bass.AP(mamba_hbm, c * TC, [[TCORE, 128], [128 * TCORE, DT], [1, TC]]))
                if c > 0:
                    w8("sync", "sVE", prev["vve_o"])
                vxtm = dma("sync", "sWX", v3(g_xtm, 4, D)[:, :, :],
                           bass.AP(xx, (3 + c * TC) * D, [[D, 128], [128 * D, 4], [1, D]]))
                if c == 0:
                    w8("tensor", "sWB", vwq2)
                else:
                    w8("tensor", "sVE", prev["evt1"])
                    w8("tensor", "sSC", prev["vsc_f"])
                for grp in range(NG):
                    par = grp % 2
                    ch_ = "sLD" if par == 0 else "sLE"
                    if grp < 2:
                        vh = prev[("vh", grp)]
                    else:
                        if par in vq:
                            w8("sync", "sVE", vq[par])
                        vh = dma("sync", ch_, g_hin[:, par * 4 * TC:(par + 1) * 4 * TC],
                                 bass.AP(h_hbm, grp * 4 * 128 * TCORE + c * TC,
                                         [[TCORE, 128], [128 * TCORE, 4], [1, TC]]))
                    w8("vector", ch_, vh)
                    w8("vector", "sPE", vhq.get(par, 0))
                    op("vector", lambda e, par=par: e.tensor_scalar_mul(
                        g_i32[:, :], g_hin[:, par * 4 * TC:(par + 1) * 4 * TC],
                        s_sc[:, 2:3]))
                    op("vector", lambda e, par=par: e.tensor_copy(
                        out=g_hq[:, par * 4 * TC:(par + 1) * 4 * TC], in_=g_i32[:, :]))
                    vve_hq = tag("vector", "sVE")
                    vq[par] = vve_hq
                    w8("tensor", "sVE", vve_hq)
                    for ki in range(4):
                        kt = grp * 4 + ki
                        for oc_ in range(DT):
                            op("tensor", lambda e, kt=kt, ki=ki, oc_=oc_, par=par: e.matmul(
                                pb[oc_][:, :], v3(w_q2, HC, D)[:, kt, oc_ * 128:(oc_ + 1) * 128],
                                g_hq[:, (par * 4 + ki) * TC:(par * 4 + ki + 1) * TC],
                                start=(kt == 0), stop=(kt == HC - 1)))
                    vhq[par] = tag("tensor", "sPE")
                vpe_f = ctr["sPE"]
                w8("scalar", "sPE", vpe_f)
                w8("scalar", "sWT", vm)
                for oc_ in range(DT):
                    op("scalar", lambda e, oc_=oc_: e.activation(
                        out=v3(g_tmp, DT, TC)[:, oc_, :], in_=pb[oc_][:, :],
                        func=AF.Identity, scale=s_sc[:, 3:4], bias=s_b2s[:, oc_:oc_ + 1]))
                vsc_f = tag("scalar", "sSC")
                prev["vsc_f"] = vsc_f
                if dbg_ffn is not None:
                    w8("sync", "sSC", vsc_f)
                    dma("sync", "sST", bass.AP(dbg_ffn, c * TC, [[TCORE, 128], [128 * TCORE, DT], [1, TC]]),
                        v3(g_tmp, DT, TC)[:, :, :])
                    dbg_quiesce()
                w8("vector", "sSC", vsc_f)
                for dt_ in range(DT):
                    op("vector", lambda e, dt_=dt_: e.tensor_add(
                        v3(g_tmp, DT, TC)[:, dt_, :], v3(g_tmp, DT, TC)[:, dt_, :],
                        v3(g_mam, DT, TC)[:, dt_, :]))
                vve_tot = tag("vector", "sVE")
                prev["vve_tot"] = vve_tot
                w8("tensor", "sVE", vve_tot)
                evt = {}
                w8("vector", "sST", prev.get("vst", 0))
                for g in range(8):
                    bank = pb[g % 2]
                    if g >= 2:
                        w8("tensor", "sVE", evt[g % 2])
                    for j in range(4):
                        dt_, tt = divmod(g * 4 + j, 4)
                        op("tensor", lambda e, bank=bank, j=j, tt=tt, dt_=dt_: e.transpose(
                            bank[:, j * 128:(j + 1) * 128],
                            v3(g_tmp, DT, TC)[:, dt_, tt * 128:(tt + 1) * 128], c_id[:, :]))
                    vpe = tag("tensor", "sPE")
                    w8("vector", "sPE", vpe)
                    for j in range(4):
                        dt_, tt = divmod(g * 4 + j, 4)
                        op("vector", lambda e, bank=bank, j=j, tt=tt, dt_=dt_: e.tensor_copy(
                            out=v3(g_ttm, 4, D)[:, tt, dt_ * 128:(dt_ + 1) * 128],
                            in_=bank[:, j * 128:(j + 1) * 128]))
                    evt[g % 2] = tag("vector", "sVE")
                prev["evt1"] = evt[1]
                w8("vector", "sWX", vxtm)
                for tt in range(4):
                    op("vector", lambda e, tt=tt: e.tensor_add(
                        v3(g_ttm, 4, D)[:, tt, :], v3(g_ttm, 4, D)[:, tt, :],
                        v3(g_xtm, 4, D)[:, tt, :]))
                vve_o = tag("vector", "sVE")
                prev["vve_o"] = vve_o
                w8("sync", "sVE", vve_o)
                dma("sync", "sST", bass.AP(out_t, c * TC * D, [[D, 128], [128 * D, 4], [1, D]]),
                    v3(g_ttm, 4, D)[:, :, :])
                prev["vst"] = ctr["sST"]

        try:
            _emit()
        except _StopBuild:
            pass
        if phase_es[0] is not None:
            phase_es[0].close()
            phase_es[0] = None
        w8("sync", "sST")
        w8("sync", "sLD")
        w8("sync", "sLE")

        with nc.Block() as block:
            @block.sync
            def _(eng):
                for f in ops["sync"]:
                    f(eng)

            @block.tensor
            def _(eng):
                for f in ops["tensor"]:
                    f(eng)

            @block.vector
            def _(eng):
                for f in ops["vector"]:
                    f(eng)

            @block.scalar
            def _(eng):
                for f in ops["scalar"]:
                    f(eng)

            @block.gpsimd
            def _(eng):
                with allow_ici_all_to_all():
                    for f in ops["gpsimd"]:
                        f(eng)
    finally:
        es.__exit__(None, None, None)

    return nc


_SEL_CACHE = {}


def _host_consts():
    if "sel" not in _SEL_CACHE:
        sel = np.zeros((64, 32 * 128), np.float32)
        for n_ in range(N):
            sel[n_, n_ * 128:(n_ + 1) * 128] = -1.0
            sel[32 + n_, n_ * 128:(n_ + 1) * 128] = -1.0
            sel[16 + n_, (16 + n_) * 128:(17 + n_) * 128] = 1.0
            sel[48 + n_, (16 + n_) * 128:(17 + n_) * 128] = 1.0
        _SEL_CACHE["sel"] = sel.astype(ml_dtypes.bfloat16)
        _SEL_CACHE["id"] = np.eye(128, dtype=np.float32)
        _SEL_CACHE["onesr"] = np.ones((1, 128), np.float32)
    return _SEL_CACHE


def make_in_maps(inputs):
    x = np.asarray(inputs["L0_attn_x"], np.float32)
    cst = _host_consts()
    common = dict(
        W_in=np.ascontiguousarray(np.asarray(inputs["L0_attn_W_in"], np.float32)),
        b_in=np.asarray(inputs["L0_attn_b_in"], np.float32).reshape(1, DH),
        conv_w=np.ascontiguousarray(np.asarray(inputs["L0_attn_conv_w"], np.float32).reshape(D, 4)),
        conv_b=np.asarray(inputs["L0_attn_conv_b"], np.float32).reshape(1, D),
        W_bc=np.ascontiguousarray(np.asarray(inputs["L0_attn_W_bc"], np.float32)),
        b_bc=np.asarray(inputs["L0_attn_b_bc"], np.float32).reshape(1, 2 * N),
        W_dt=np.ascontiguousarray(np.asarray(inputs["L0_attn_W_dt"], np.float32)),
        b_dt=np.asarray(inputs["L0_attn_b_dt"], np.float32).reshape(1, D),
        Dvec=np.asarray(inputs["L0_attn_D"], np.float32).reshape(1, D),
        W_out=np.ascontiguousarray(np.asarray(inputs["L0_attn_W_out"], np.float32)),
        b_out=np.asarray(inputs["L0_attn_b_out"], np.float32).reshape(1, D),
        ln_g=np.asarray(inputs["L0_attn_ln_g"], np.float32).reshape(1, D),
        ln_b=np.asarray(inputs["L0_attn_ln_b"], np.float32).reshape(1, D),
        w1=np.ascontiguousarray(np.asarray(inputs["L0_ffn_w1"], np.float32)),
        b1=np.asarray(inputs["L0_ffn_b1"], np.float32).reshape(1, H),
        w2=np.ascontiguousarray(np.asarray(inputs["L0_ffn_w2"], np.float32)),
        b2=np.asarray(inputs["L0_ffn_b2"], np.float32).reshape(1, D),
        ident=cst["id"], ones_r=cst["onesr"], selBC=cst["sel"],
    )
    A_log = np.asarray(inputs["L0_attn_A_log"], np.float32)
    in_maps = []
    for k in range(8):
        b_, hh = divmod(k, 2)
        base = hh * TCORE
        if hh == 0:
            lead = np.zeros((3, D), np.float32)
            lm = np.zeros((128, 1), np.float32)
        else:
            lead = x[b_, base - 3:base, :]
            lm = np.ones((128, 1), np.float32)
        xxk = np.concatenate([lead, x[b_, base:base + TCORE, :]], axis=0)
        m = dict(common)
        m["xx"] = np.ascontiguousarray(xxk)
        m["A_log_sl"] = np.ascontiguousarray(A_log[k * 128:(k + 1) * 128, :])
        m["leadmask"] = lm
        in_maps.append(m)
    return in_maps


def kernel(**inputs):
    in_maps = make_in_maps(inputs)
    nc = build_nc()
    res = run_bass_kernel_spmd(nc, in_maps, core_ids=list(range(8)))
    outs = np.stack([np.asarray(r["out"]) for r in res.results])
    return np.ascontiguousarray(outs.reshape(B, T, D).astype(np.float32))


# version 15
# speedup vs baseline: 1.0232x; 1.0056x over previous
import numpy as np
import ml_dtypes
from contextlib import ExitStack
import concourse.bass as bass
import concourse.mybir as mybir
from concourse.bass_utils import run_bass_kernel_spmd
from concourse.aot_env import allow_ici_all_to_all

F32 = mybir.dt.float32
BF16 = mybir.dt.bfloat16
I32 = mybir.dt.int32
AF = mybir.ActivationFunctionType
ALU = mybir.AluOpType
AX = mybir.AxisListType

B, T, D, DH, H, N = 4, 8192, 1024, 2048, 4096, 16
TCORE = 4096
CH = 8
TC = 512
DT = 8
HC = 32
OC = 16
LN_EPS = 1e-5
QP = 127.0
MIN_RANGE = 1e-6

ENGS = ("sync", "tensor", "vector", "scalar", "gpsimd")


class _StopBuild(Exception):
    pass


def build_nc(debug=(), upto="all"):
    nc = bass.Bass("TRN2", target_bir_lowering=False)

    def din(name, shape, dt=F32):
        return nc.dram_tensor(name, shape, dt, kind="ExternalInput")

    xx = din("xx", [TCORE + 3, D])
    W_in = din("W_in", [D, DH]); b_in = din("b_in", [1, DH])
    conv_w = din("conv_w", [D, 4]); conv_b = din("conv_b", [1, D])
    W_bc = din("W_bc", [D, 2 * N]); b_bc = din("b_bc", [1, 2 * N])
    W_dt = din("W_dt", [D, D]); b_dt = din("b_dt", [1, D])
    A_log_sl = din("A_log_sl", [128, N])
    Dvec = din("Dvec", [1, D])
    W_out = din("W_out", [D, D]); b_out = din("b_out", [1, D])
    ln_g = din("ln_g", [1, D]); ln_b = din("ln_b", [1, D])
    w1 = din("w1", [H, D]); b1 = din("b1", [1, H])
    w2 = din("w2", [D, H]); b2 = din("b2", [1, D])
    ident = din("ident", [128, 128])
    ones_r = din("ones_r", [1, 128])
    selBC = din("selBC", [64, 32 * 128], BF16)
    leadmask = din("leadmask", [128, 1])

    out_t = nc.dram_tensor("out", [TCORE, D], F32, kind="ExternalOutput")
    d_dbg = {}

    def dbg(name, shape):
        if name in debug:
            d_dbg[name] = nc.dram_tensor("dbg_" + name, shape, F32, kind="ExternalOutput")
            return d_dbg[name]
        return None

    xn_hbm = nc.dram_tensor("xn_hbm", [DT, 128, TCORE], F32)
    gate_hbm = nc.dram_tensor("gate_hbm", [DT, 128, TCORE], F32)
    h_hbm = nc.dram_tensor("h_hbm", [HC, 128, TCORE], F32)
    mamba_hbm = nc.dram_tensor("mamba_hbm", [DT, 128, TCORE], F32)
    w1qT_hbm = nc.dram_tensor("w1qT_hbm", [DT, 128, H], BF16)
    w2qT_hbm = nc.dram_tensor("w2qT_hbm", [HC, 128, D], BF16)
    a2a_in = [nc.dram_tensor(f"a2a_in{c}", [8, 2, 128, TC], F32) for c in range(CH)]
    a2a_out = [nc.dram_tensor(f"a2a_out{c}", [8, 2, 128, TC], F32) for c in range(CH)]
    bc_in = [nc.dram_tensor(f"bc_in{c}", [32, TC], F32) for c in range(CH)]
    bc_out = [nc.dram_tensor(f"bc_out{c}", [8, 32, TC], F32) for c in range(CH)]
    ya_in = [nc.dram_tensor(f"ya_in{c}", [8, 128, TC], F32) for c in range(CH)]
    ya_out = [nc.dram_tensor(f"ya_out{c}", [8, 128, TC], F32) for c in range(CH)]
    ar1_in = nc.dram_tensor("ar1_in", [128, 1], F32)
    ar1_out = nc.dram_tensor("ar1_out", [128, 1], F32)
    ar2_in = nc.dram_tensor("ar2_in", [128, 1], F32)
    ar2_out = nc.dram_tensor("ar2_out", [128, 1], F32)

    ops = {e: [] for e in ENGS}
    ctr = {}

    es = ExitStack()
    es.__enter__()
    try:
        def sbuf(name, shape, dt=F32):
            return es.enter_context(nc.sbuf_tensor(name, shape, dt))

        c_bin = sbuf("c_bin", [128, OC])
        c_convw = sbuf("c_convw", [128, DT * 4])
        c_convb = sbuf("c_convb", [128, DT])
        c_bbc = sbuf("c_bbc", [32, 1])
        c_nbdt = sbuf("c_nbdt", [128, DT])
        c_negA = sbuf("c_negA", [128, N])
        c_D = sbuf("c_D", [128, DT])
        c_bout = sbuf("c_bout", [128, DT])
        c_lng = sbuf("c_lng", [128, DT])
        c_lnb = sbuf("c_lnb", [128, DT])
        c_b1 = sbuf("c_b1", [128, HC])
        c_b2 = sbuf("c_b2", [128, DT])
        c_id = sbuf("c_id", [128, 128])
        c_onesr = sbuf("c_onesr", [1, 128])
        c_ones128 = sbuf("c_ones128", [128, 128])
        c_sel = sbuf("c_sel", [64, 32 * 128], BF16)
        c_lead = sbuf("c_lead", [128, 1])
        s_sc = sbuf("s_sc", [128, 8])
        s_b1s = sbuf("s_b1s", [128, HC])
        s_b2s = sbuf("s_b2s", [128, DT])
        a_amax = sbuf("a_amax", [128, 4])
        a_stat = sbuf("a_stat", [128, 32])
        s_state = sbuf("s_state", [128, 4 * N])
        g_tr = sbuf("g_tr", [1, 128])

        pb = [es.enter_context(nc.psum_tensor(f"pb{i}", [128, 512], F32))
              for i in range(8)]

        sems = {}
        for s in ("sLD", "sLE", "sLF", "sLG", "sST", "sPE", "sSC", "sVE", "sCC", "sWT", "sWB", "sWX"):
            sems[s] = es.enter_context(nc.semaphore(s))
            ctr[s] = 0

        def w8(eng, s, v=None):
            v = ctr[s] if v is None else v
            if v > 0:
                ops[eng].append(lambda e, h=sems[s], v=v: e.wait_ge(h, v))

        def dma(eng, s, o, i, slow=False):
            ctr[s] += 16
            ops[eng].append(lambda e, o=o, i=i, h=sems[s], slow=slow: e.dma_start(
                out=o, in_=i, allow_slow_non_contiguous=slow).then_inc(h, 16))
            return ctr[s]

        def tag(eng, s):
            ctr[s] += 1
            last = ops[eng].pop()
            ops[eng].append(lambda e, last=last, h=sems[s]: last(e).then_inc(h, 1))
            return ctr[s]

        def op(eng, fn):
            ops[eng].append(fn)

        def dbg_quiesce():
            for e_ in ("vector", "scalar", "tensor"):
                w8(e_, "sST")

        def dr(eng):
            ops[eng].append(lambda e: e.drain())

        def v3(t, a, bdim):
            return t[:, :].rearrange("p (a b) -> p a b", b=bdim)

        phase_es = [None]

        def phase_begin():
            if phase_es[0] is not None:
                phase_es[0].close()
            phase_es[0] = ExitStack()
            w8("sync", "sSC")
            w8("sync", "sVE")
            w8("sync", "sPE")
            w8("sync", "sST")

        def psbuf(name, shape, dt=F32):
            return phase_es[0].enter_context(nc.sbuf_tensor(name, shape, dt))

        def _emit():
            def ld_pm(dst, src, ncols):
                return dma("sync", "sLD", dst[:, 0:ncols],
                           bass.AP(src, 0, [[1, 128], [128, ncols]]), slow=True)

            ld_pm(c_bin, b_in, OC)
            ld_pm(c_convb, conv_b, DT)
            ld_pm(c_nbdt, b_dt, DT)
            ld_pm(c_D, Dvec, DT)
            ld_pm(c_bout, b_out, DT)
            ld_pm(c_lng, ln_g, DT)
            ld_pm(c_lnb, ln_b, DT)
            ld_pm(c_b1, b1, HC)
            ld_pm(c_b2, b2, DT)
            dma("sync", "sLD", c_bbc[:, 0:1], bass.AP(b_bc, 0, [[1, 32], [1, 1]]), slow=True)
            dma("sync", "sLD", v3(c_convw, DT, 4),
                bass.AP(conv_w, 0, [[4, 128], [512, DT], [1, 4]]))
            dma("sync", "sLD", c_negA[:, :], A_log_sl[:, :])
            dma("sync", "sLD", c_id[:, :], ident[:, :])
            dma("sync", "sLD", c_onesr[:, :], ones_r[:, :])
            dma("sync", "sLD", c_sel[:, :], selBC[:, :])
            vld = dma("sync", "sLD", c_lead[:, :], leadmask[:, :])
            w8("scalar", "sLD", vld)
            op("scalar", lambda e: e.activation(out=c_negA[:, :], in_=c_negA[:, :], func=AF.Exp))
            tag("scalar", "sSC")
            w8("vector", "sLD", vld)
            op("vector", lambda e: e.tensor_scalar_mul(c_nbdt[:, :], c_nbdt[:, :], -1.0))
            op("vector", lambda e: e.memset(a_amax[:, 0:4], 0.0))
            op("vector", lambda e: e.memset(c_ones128[:, :], 1.0))
            op("vector", lambda e: e.memset(s_state[:, :], 0.0))
            tag("vector", "sVE")
            if upto == "p0":
                raise _StopBuild

            phase_begin()
            q_w = psbuf("q_w", [128, 32 * 1024])
            q_asm = psbuf("q_asm", [128, H], BF16)
            q_i32 = psbuf("q_i32", [128, 128], I32)

            def prep_w(which):
                wsrc, rows, cols = (w1, H, D) if which == 0 else (w2, D, H)
                rchunks = rows // 128
                scol = 4 + which
                wscol = 6 + which
                dsthbm = w1qT_hbm if which == 0 else w2qT_hbm
                nout = cols // 128
                vw = dma("sync", "sLD", q_w[:, 0:rchunks * cols],
                         bass.AP(wsrc, 0, [[cols, 128], [128 * cols, rchunks], [1, cols]]))
                w8("vector", "sLD", vw)
                op("vector", lambda e, rchunks=rchunks, cols=cols: e.tensor_reduce(
                    a_amax[:, 0:1], q_w[:, 0:rchunks * cols], AX.X, ALU.max,
                    apply_absolute_value=True))
                vve = tag("vector", "sVE")
                w8("sync", "sVE", vve)
                vtr = dma("sync", "sWT", g_tr[0:1, 0:128], a_amax[:, 0:1])
                w8("tensor", "sWT", vtr)
                op("tensor", lambda e: e.matmul(pb[6][:, 0:128], c_onesr[0:1, :], g_tr[0:1, :],
                                                start=True, stop=True))
                vpe = tag("tensor", "sPE")
                w8("vector", "sPE", vpe)
                op("vector", lambda e, wscol=wscol: e.tensor_reduce(
                    s_sc[:, wscol:wscol + 1], pb[6][:, 0:128], AX.X, ALU.max))
                dr("vector")
                op("vector", lambda e, wscol=wscol: e.tensor_scalar_max(
                    s_sc[:, wscol:wscol + 1], s_sc[:, wscol:wscol + 1], MIN_RANGE))
                dr("vector")
                op("vector", lambda e, scol=scol, wscol=wscol: e.reciprocal(
                    s_sc[:, scol:scol + 1], s_sc[:, wscol:wscol + 1]))
                dr("vector")
                op("vector", lambda e, scol=scol: e.tensor_scalar_mul(
                    s_sc[:, scol:scol + 1], s_sc[:, scol:scol + 1], QP))
                op("vector", lambda e, wscol=wscol: e.tensor_scalar_mul(
                    s_sc[:, wscol:wscol + 1], s_sc[:, wscol:wscol + 1], 1.0 / QP))
                dr("vector")
                vve0 = tag("vector", "sVE")
                if upto == "prep_amax":
                    raise _StopBuild
                evac_done = {}
                for ocq in range(nout):
                    for rc in range(rchunks):
                        bank = pb[rc % 2]
                        w8("tensor", "sVE", evac_done.get(rc % 2, vve0))
                        op("tensor", lambda e, rc=rc, ocq=ocq, bank=bank, cols=cols: e.transpose(
                            bank[:, 0:128],
                            q_w[:, rc * cols + ocq * 128: rc * cols + ocq * 128 + 128],
                            c_id[:, :]))
                        vt = tag("tensor", "sPE")
                        w8("vector", "sPE", vt)
                        op("vector", lambda e, bank=bank, scol=scol: e.tensor_scalar_mul(
                            q_i32[:, :], bank[:, 0:128], s_sc[:, scol:scol + 1]))
                        dr("vector")
                        op("vector", lambda e, rc=rc: e.tensor_copy(
                            out=q_asm[:, rc * 128: rc * 128 + 128], in_=q_i32[:, :]))
                        evac_done[rc % 2] = tag("vector", "sVE")
                    w8("sync", "sVE", evac_done[(rchunks - 1) % 2])
                    vst = dma("sync", "sST", dsthbm[ocq, :, 0:rows], q_asm[:, 0:rows])
                    w8("vector", "sST", vst)
                    if upto == "prep_q1":
                        raise _StopBuild

            prep_w(0)
            w8("sync", "sVE")
            prep_w(1)
            if upto == "prep":
                raise _StopBuild

            phase_begin()
            a_xtm = psbuf("a_xtm", [128, 4 * D])
            a_xntm = psbuf("a_xntm", [128, 4 * D])
            a_xnfm = psbuf("a_xnfm", [128, DT * TC])
            a_xpad = psbuf("a_xpad", [128, DT * (TC + 4)])
            a_carry = psbuf("a_carry", [128, DT * 4])
            a_out1 = psbuf("a_out1", [128, DT * TC])
            a_xconv = psbuf("a_xconv", [128, DT * TC])
            w_a = psbuf("w_a", [128, DT * DH])

            vWa = dma("sync", "sWB", w_a[:, :],
                      bass.AP(W_in, 0, [[DH, 128], [128 * DH, DT], [1, DH]]))

            dbg_xn = dbg("xn", [DT, 128, TCORE])
            dbg_xconv = dbg("xconv", [CH, DT, 128, TC])
            dbg_dtneg = dbg("dtneg", [CH, DT, 128, TC])
            dbg_bc = dbg("bc", [CH, 32, TC])
            dbg_gate = dbg("gate", [DT, 128, TCORE])
            dbg_mam = dbg("mam", [DT, 128, TCORE])
            dbg_xntm = dbg("xntm", [CH, 128, 4 * D])
            dbg_stat = dbg("stat", [CH, 128, 32])

            def ln_chunk(nt, xoff):
                w8("sync", "sSC")
                v = dma("sync", "sLD", v3(a_xtm, 4, D)[:, 0:nt, :],
                        bass.AP(xx, xoff * D, [[D, 128], [128 * D, nt], [1, D]]))
                w8("vector", "sLD", v)
                op("vector", lambda e, nt=nt: e.tensor_reduce(
                    a_stat[:, 0:nt], v3(a_xtm, 4, D)[:, 0:nt, :], AX.X, ALU.add))
                tag("vector", "sVE")
                w8("scalar", "sLD", v)
                for tt in range(nt):
                    op("scalar", lambda e, tt=tt: e.activation(
                        out=v3(a_xntm, 4, D)[:, tt, :], in_=v3(a_xtm, 4, D)[:, tt, :],
                        func=AF.Square, accum_out=a_stat[:, 8 + tt:9 + tt]))
                vsc = tag("scalar", "sSC")
                w8("vector", "sSC", vsc)
                op("vector", lambda e, nt=nt: e.tensor_scalar_mul(a_stat[:, 16:16 + nt], a_stat[:, 0:nt], 1.0 / D))
                dr("vector")
                op("vector", lambda e, nt=nt: e.tensor_mul(a_stat[:, 24:24 + nt], a_stat[:, 16:16 + nt], a_stat[:, 16:16 + nt]))
                op("vector", lambda e, nt=nt: e.tensor_scalar_mul(a_stat[:, 8:8 + nt], a_stat[:, 8:8 + nt], 1.0 / D))
                dr("vector")
                op("vector", lambda e, nt=nt: e.tensor_sub(a_stat[:, 8:8 + nt], a_stat[:, 8:8 + nt], a_stat[:, 24:24 + nt]))
                dr("vector")
                op("vector", lambda e, nt=nt: e.tensor_scalar_add(a_stat[:, 8:8 + nt], a_stat[:, 8:8 + nt], LN_EPS))
                vve = tag("vector", "sVE")
                w8("scalar", "sVE", vve)
                op("scalar", lambda e, nt=nt: e.activation(
                    out=a_stat[:, 28:28 + nt], in_=a_stat[:, 8:8 + nt], func=AF.Sqrt))
                vsc = tag("scalar", "sSC")
                w8("vector", "sSC", vsc)
                op("vector", lambda e, nt=nt: e.reciprocal(a_stat[:, 24:24 + nt], a_stat[:, 28:28 + nt]))
                dr("vector")
                op("vector", lambda e, nt=nt: e.tensor_mul(a_stat[:, 16:16 + nt], a_stat[:, 16:16 + nt], a_stat[:, 24:24 + nt]))
                dr("vector")
                op("vector", lambda e, nt=nt: e.tensor_scalar_mul(a_stat[:, 16:16 + nt], a_stat[:, 16:16 + nt], -1.0))
                vve = tag("vector", "sVE")
                w8("scalar", "sVE", vve)
                for tt in range(nt):
                    op("scalar", lambda e, tt=tt: e.activation(
                        out=v3(a_xntm, 4, D)[:, tt, :], in_=v3(a_xtm, 4, D)[:, tt, :], func=AF.Identity,
                        scale=a_stat[:, 24 + tt:25 + tt], bias=a_stat[:, 16 + tt:17 + tt]))
                return tag("scalar", "sSC")

            vsc = ln_chunk(1, 0)
            w8("tensor", "sSC", vsc)
            w8("tensor", "sWB", vWa)
            for dt_ in range(DT):
                op("tensor", lambda e, dt_=dt_: e.transpose(
                    pb[4][:, dt_ * 4: dt_ * 4 + 3],
                    v3(a_xntm, 4, D)[0:3, 0, dt_ * 128:(dt_ + 1) * 128], c_id[0:3, 0:3]))
            vpe = tag("tensor", "sPE")
            w8("vector", "sPE", vpe)
            op("vector", lambda e: e.tensor_copy(out=a_carry[:, 0:DT * 4], in_=pb[4][:, 0:DT * 4]))
            vve = tag("vector", "sVE")
            w8("tensor", "sVE", vve)
            for oc_ in range(DT):
                for kt in range(DT):
                    op("tensor", lambda e, oc_=oc_, kt=kt: e.matmul(
                        pb[5][:, oc_ * 4: oc_ * 4 + 3],
                        v3(w_a, DT, DH)[:, kt, oc_ * 128:(oc_ + 1) * 128],
                        a_carry[:, kt * 4:kt * 4 + 3],
                        start=(kt == 0 and oc_ == 0), stop=(kt == DT - 1 and oc_ == DT - 1)))
            vpe = tag("tensor", "sPE")
            w8("scalar", "sPE", vpe)
            for oc_ in range(DT):
                op("scalar", lambda e, oc_=oc_: e.activation(
                    out=v3(a_xpad, DT, TC + 4)[:, oc_, 0:3], in_=pb[5][:, oc_ * 4:oc_ * 4 + 3],
                    func=AF.Identity, bias=c_bin[:, oc_:oc_ + 1]))
            vsc_lead = tag("scalar", "sSC")
            w8("vector", "sSC", vsc_lead)
            for dt_ in range(DT):
                op("vector", lambda e, dt_=dt_: e.tensor_scalar_mul(
                    v3(a_carry, DT, 4)[:, dt_, 0:3], v3(a_xpad, DT, TC + 4)[:, dt_, 0:3],
                    c_lead[:, 0:1]))
            tag("vector", "sVE")
            evac = {0: ("sVE", ctr["sVE"]), 1: ("sSC", vsc_lead)}
            if upto == "lead":
                raise _StopBuild

            for c in range(CH):
                st_prev = ctr["sST"]
                vsc = ln_chunk(4, 3 + c * TC)
                if dbg_xntm is not None:
                    w8("sync", "sSC", vsc)
                    dma("sync", "sST", bass.AP(dbg_xntm, c * 128 * 4 * D, [[4 * D, 128], [1, 4 * D]]),
                        a_xntm[:, :])
                    dma("sync", "sST", bass.AP(dbg_stat, c * 128 * 32, [[32, 128], [1, 32]]),
                        a_stat[:, :])
                    dbg_quiesce()
                w8("vector", "sSC", vsc)
                op("vector", lambda e: e.tensor_reduce(
                    a_stat[:, 0:4], v3(a_xntm, 4, D)[:, :, :], AX.X, ALU.max, apply_absolute_value=True))
                dr("vector")
                op("vector", lambda e: e.tensor_reduce(
                    a_stat[:, 4:5], a_stat[:, 0:4], AX.X, ALU.max))
                dr("vector")
                op("vector", lambda e: e.tensor_max(a_amax[:, 2:3], a_amax[:, 2:3], a_stat[:, 4:5]))
                tag("vector", "sVE")
                w8("vector", "sST", st_prev)
                w8("tensor", "sSC", vsc)
                for g in range(8):
                    bank = pb[4 + (g % 2)]
                    w8("tensor", *evac[g % 2])
                    for j in range(4):
                        tt, dt_ = divmod(g * 4 + j, DT)
                        op("tensor", lambda e, bank=bank, j=j, tt=tt, dt_=dt_: e.transpose(
                            bank[:, j * 128:(j + 1) * 128],
                            v3(a_xntm, 4, D)[:, tt, dt_ * 128:(dt_ + 1) * 128], c_id[:, :]))
                    vpe = tag("tensor", "sPE")
                    w8("vector", "sPE", vpe)
                    for j in range(4):
                        tt, dt_ = divmod(g * 4 + j, DT)
                        op("vector", lambda e, bank=bank, j=j, tt=tt, dt_=dt_: e.tensor_copy(
                            out=v3(a_xnfm, DT, TC)[:, dt_, tt * 128:(tt + 1) * 128],
                            in_=bank[:, j * 128:(j + 1) * 128]))
                    evac[g % 2] = ("sVE", tag("vector", "sVE"))
                vve_xn = ctr["sVE"]
                w8("sync", "sVE", vve_xn)
                dma("sync", "sST", bass.AP(xn_hbm, c * TC, [[TCORE, 128], [128 * TCORE, DT], [1, TC]]),
                    v3(a_xnfm, DT, TC)[:, :, :])
                if dbg_xn is not None:
                    dma("sync", "sST", bass.AP(dbg_xn, c * TC, [[TCORE, 128], [128 * TCORE, DT], [1, TC]]),
                        v3(a_xnfm, DT, TC)[:, :, :])

                w8("tensor", "sVE", vve_xn)
                evp = {}
                for oc_ in range(OC):
                    bank = pb[oc_ % 2]
                    if oc_ >= 2:
                        w8("tensor", "sSC", evp[oc_ % 2])
                    for kt in range(DT):
                        op("tensor", lambda e, oc_=oc_, kt=kt, bank=bank: e.matmul(
                            bank[:, :], v3(w_a, DT, DH)[:, kt, oc_ * 128:(oc_ + 1) * 128],
                            v3(a_xnfm, DT, TC)[:, kt, :], start=(kt == 0), stop=(kt == DT - 1)))
                    vpe = tag("tensor", "sPE")
                    w8("scalar", "sPE", vpe)
                    if oc_ == 0:
                        w8("scalar", "sST", st_prev)
                    if oc_ < DT:
                        op("scalar", lambda e, oc_=oc_, bank=bank: e.activation(
                            out=v3(a_xpad, DT, TC + 4)[:, oc_, 3:3 + TC], in_=bank[:, :],
                            func=AF.Identity, bias=c_bin[:, oc_:oc_ + 1]))
                    else:
                        op("scalar", lambda e, oc_=oc_, bank=bank: e.activation(
                            out=v3(a_out1, DT, TC)[:, oc_ - DT, :], in_=bank[:, :],
                            func=AF.Silu, bias=c_bin[:, oc_:oc_ + 1]))
                    evp[oc_ % 2] = tag("scalar", "sSC")
                vsc_proj = ctr["sSC"]
                w8("sync", "sSC", vsc_proj)
                dma("sync", "sST", bass.AP(gate_hbm, c * TC, [[TCORE, 128], [128 * TCORE, DT], [1, TC]]),
                    v3(a_out1, DT, TC)[:, :, :])
                if dbg_gate is not None:
                    dma("sync", "sST", bass.AP(dbg_gate, c * TC, [[TCORE, 128], [128 * TCORE, DT], [1, TC]]),
                        v3(a_out1, DT, TC)[:, :, :])

                w8("vector", "sSC", vsc_proj)
                w8("vector", "sST", st_prev)
                for dt_ in range(DT):
                    op("vector", lambda e, dt_=dt_: e.tensor_copy(
                        out=v3(a_xpad, DT, TC + 4)[:, dt_, 0:3], in_=v3(a_carry, DT, 4)[:, dt_, 0:3]))
                for dt_ in range(DT):
                    op("vector", lambda e, dt_=dt_: e.tensor_scalar_mul(
                        v3(a_xconv, DT, TC)[:, dt_, :], v3(a_xpad, DT, TC + 4)[:, dt_, 0:TC],
                        v3(c_convw, DT, 4)[:, dt_, 0:1]))
                    for k in range(1, 4):
                        op("vector", lambda e, dt_=dt_, k=k: e.scalar_tensor_tensor(
                            out=v3(a_xconv, DT, TC)[:, dt_, :],
                            in0=v3(a_xpad, DT, TC + 4)[:, dt_, k:k + TC],
                            scalar=v3(c_convw, DT, 4)[:, dt_, k:k + 1],
                            in1=v3(a_xconv, DT, TC)[:, dt_, :],
                            op0=ALU.mult, op1=ALU.add))
                    op("vector", lambda e, dt_=dt_: e.tensor_copy(
                        out=v3(a_carry, DT, 4)[:, dt_, 0:3],
                        in_=v3(a_xpad, DT, TC + 4)[:, dt_, TC:3 + TC]))
                vve = tag("vector", "sVE")
                w8("scalar", "sVE", vve)
                for dt_ in range(DT):
                    op("scalar", lambda e, dt_=dt_: e.activation(
                        out=v3(a_xconv, DT, TC)[:, dt_, :], in_=v3(a_xconv, DT, TC)[:, dt_, :],
                        func=AF.Silu, bias=c_convb[:, dt_:dt_ + 1]))
                vsc_conv = tag("scalar", "sSC")
                w8("sync", "sSC", vsc_conv)
                dma("sync", "sST", bass.AP(a2a_in[c], 128 * TC, [[TC, 128], [2 * 128 * TC, DT], [1, TC]]),
                    v3(a_xconv, DT, TC)[:, :, :])
                if dbg_xconv is not None:
                    dma("sync", "sST", bass.AP(dbg_xconv, c * DT * 128 * TC, [[TC, 128], [128 * TC, DT], [1, TC]]),
                        v3(a_xconv, DT, TC)[:, :, :])
            if upto == "a1":
                raise _StopBuild

            def global_scale(amax_col, arin, arout, rcol, sccol):
                w8("sync", "sVE")
                dma("sync", "sST", arin[:, :], amax_col)
                w8("gpsimd", "sST")
                ctr["sCC"] += 1
                op("gpsimd", lambda e, arin=arin, arout=arout: e.collective_compute(
                    "AllReduce", ALU.max, ins=[arin[:, :]], outs=[arout[:, :]],
                    replica_groups=[list(range(8))]).then_inc(sems["sCC"], 1))
                w8("sync", "sCC")
                vldx = dma("sync", "sWT", g_tr[0:1, 0:128], arout[:, :])
                w8("tensor", "sWT", vldx)
                op("tensor", lambda e: e.matmul(pb[6][:, 0:128], c_onesr[0:1, :], g_tr[0:1, :],
                                                start=True, stop=True))
                vpe = tag("tensor", "sPE")
                w8("vector", "sPE", vpe)
                op("vector", lambda e, sccol=sccol: e.tensor_reduce(
                    s_sc[:, sccol:sccol + 1], pb[6][:, 0:128], AX.X, ALU.max))
                dr("vector")
                op("vector", lambda e, sccol=sccol: e.tensor_scalar_max(
                    s_sc[:, sccol:sccol + 1], s_sc[:, sccol:sccol + 1], MIN_RANGE))
                dr("vector")
                op("vector", lambda e, rcol=rcol, sccol=sccol: e.reciprocal(
                    s_sc[:, rcol:rcol + 1], s_sc[:, sccol:sccol + 1]))
                dr("vector")
                op("vector", lambda e, rcol=rcol: e.tensor_scalar_mul(
                    s_sc[:, rcol:rcol + 1], s_sc[:, rcol:rcol + 1], QP))
                op("vector", lambda e, sccol=sccol: e.tensor_scalar_mul(
                    s_sc[:, sccol:sccol + 1], s_sc[:, sccol:sccol + 1], 1.0 / QP))
                dr("vector")
                tag("vector", "sVE")

            global_scale(a_amax[:, 2:3], ar1_in, ar1_out, 0, 3)
            op("vector", lambda e: e.tensor_mul(s_sc[:, 1:2], s_sc[:, 6:7], s_sc[:, 3:4]))
            dr("vector")
            op("vector", lambda e: e.tensor_scalar_mul(s_b1s[:, :], c_b1[:, :], s_sc[:, 1:2]))
            tag("vector", "sVE")
            if upto == "ar1":
                raise _StopBuild

            phase_begin()
            a_xc2 = psbuf("a_xc2", [128, DT * TC])
            a_out2 = psbuf("a_out2", [128, DT * TC])
            a_bc = psbuf("a_bc", [32, TC])
            w_b = psbuf("w_b", [128, DT * D])
            w_c = psbuf("w_c", [128, DT * 2 * N])

            dma("sync", "sWB", w_b[:, :],
                bass.AP(W_dt, 0, [[D, 128], [128 * D, DT], [1, D]]))
            vWb = dma("sync", "sWB", w_c[:, :],
                      bass.AP(W_bc, 0, [[2 * N, 128], [128 * 2 * N, DT], [1, 2 * N]]))

            for c in range(CH):
                w8("sync", "sSC")
                w8("sync", "sVE")
                w8("sync", "sST")
                vxc = dma("sync", "sLD", v3(a_xc2, DT, TC)[:, :, :],
                          bass.AP(a2a_in[c], 128 * TC, [[TC, 128], [2 * 128 * TC, DT], [1, TC]]))
                w8("tensor", "sLD", vxc)
                w8("tensor", "sWB", vWb)
                evd = {}
                for oc_ in range(DT):
                    bank = pb[oc_ % 2]
                    if oc_ >= 2:
                        w8("tensor", "sSC", evd[oc_ % 2])
                    for kt in range(DT):
                        op("tensor", lambda e, oc_=oc_, kt=kt, bank=bank: e.matmul(
                            bank[:, :], v3(w_b, DT, D)[:, kt, oc_ * 128:(oc_ + 1) * 128],
                            v3(a_xc2, DT, TC)[:, kt, :], start=(kt == 0), stop=(kt == DT - 1)))
                    vpe = tag("tensor", "sPE")
                    w8("scalar", "sPE", vpe)
                    op("scalar", lambda e, oc_=oc_, bank=bank: e.activation(
                        out=v3(a_out2, DT, TC)[:, oc_, :], in_=bank[:, :],
                        func=AF.Sigmoid, scale=-1.0, bias=c_nbdt[:, oc_:oc_ + 1]))
                    op("scalar", lambda e, oc_=oc_: e.activation(
                        out=v3(a_out2, DT, TC)[:, oc_, :], in_=v3(a_out2, DT, TC)[:, oc_, :],
                        func=AF.Ln))
                    evd[oc_ % 2] = tag("scalar", "sSC")
                bank = pb[2]
                for kt in range(DT):
                    op("tensor", lambda e, kt=kt, bank=bank: e.matmul(
                        bank[0:32, :], v3(w_c, DT, 2 * N)[:, kt, :],
                        v3(a_xc2, DT, TC)[:, kt, :], start=(kt == 0), stop=(kt == DT - 1)))
                vpe_bc = tag("tensor", "sPE")
                w8("scalar", "sPE", vpe_bc)
                op("scalar", lambda e, bank=bank: e.activation(
                    out=a_bc[:, :], in_=bank[0:32, :], func=AF.Identity, bias=c_bbc[:, 0:1]))
                vsc_all = tag("scalar", "sSC")
                w8("vector", "sSC", vsc_all)
                op("vector", lambda e: e.tensor_scalar_max(a_out2[:, :], a_out2[:, :], -80.0))
                vve_dt = tag("vector", "sVE")
                w8("sync", "sVE", vve_dt)
                dma("sync", "sST", bass.AP(a2a_in[c], 0, [[TC, 128], [2 * 128 * TC, DT], [1, TC]]),
                    v3(a_out2, DT, TC)[:, :, :])
                dma("sync", "sST", bc_in[c][:, :], a_bc[:, :])
                if dbg_dtneg is not None:
                    dma("sync", "sST", bass.AP(dbg_dtneg, c * DT * 128 * TC, [[TC, 128], [128 * TC, DT], [1, TC]]),
                        v3(a_out2, DT, TC)[:, :, :])
                if dbg_bc is not None:
                    dma("sync", "sST", bass.AP(dbg_bc, c * 32 * TC, [[TC, 32], [1, TC]]), a_bc[:, :])
                vst = ctr["sST"]
                w8("gpsimd", "sST", vst)
                ctr["sCC"] += 1
                op("gpsimd", lambda e, c=c: e.collective_compute(
                    "AllToAll", ALU.bypass, ins=[a2a_in[c][:, :, :, :]],
                    outs=[a2a_out[c][:, :, :, :]], replica_groups=[list(range(8))]).then_inc(sems["sCC"], 1))
                ctr["sCC"] += 1
                op("gpsimd", lambda e, c=c: e.collective_compute(
                    "AllGather", ALU.bypass, ins=[bc_in[c][:, :]],
                    outs=[bc_out[c][:, :, :]], replica_groups=[list(range(8))]).then_inc(sems["sCC"], 1))
            vcc_a2 = ctr["sCC"]
            if upto == "a2":
                raise _StopBuild

            phase_begin()
            f_xn = psbuf("f_xn", [128, DT * TC])
            f_xq = psbuf("f_xq", [128, DT * TC], BF16)
            f_i32 = psbuf("f_i32", [128, TC], I32)
            f_h = psbuf("f_h", [128, 8 * TC])
            w_q1 = psbuf("w_q1", [128, DT * H], BF16)

            vwq = dma("sync", "sWB", w_q1[:, :],
                      bass.AP(w1qT_hbm, 0, [[H, 128], [128 * H, DT], [1, H]]))
            dbg_hff = dbg("hff", [HC, 128, TCORE])
            dbg_sc = dbg("sc", [128, 8])
            vpe_ffn_prev = 0
            for c in range(CH):
                stp = ctr["sST"]
                w8("sync", "sVE")
                vx = dma("sync", "sLD", v3(f_xn, DT, TC)[:, :, :],
                         bass.AP(xn_hbm, c * TC, [[TCORE, 128], [128 * TCORE, DT], [1, TC]]))
                w8("vector", "sLD", vx)
                w8("vector", "sPE", vpe_ffn_prev)
                for dt_ in range(DT):
                    op("vector", lambda e, dt_=dt_: e.tensor_scalar_mul(
                        f_i32[:, :], v3(f_xn, DT, TC)[:, dt_, :], s_sc[:, 0:1]))
                    op("vector", lambda e, dt_=dt_: e.tensor_copy(
                        out=v3(f_xq, DT, TC)[:, dt_, :], in_=f_i32[:, :]))
                vve_q = tag("vector", "sVE")
                w8("tensor", "sVE", vve_q)
                w8("tensor", "sWB", vwq)
                evf = {}
                for hc_ in range(HC):
                    bank = pb[hc_ % 2]
                    if hc_ >= 2:
                        w8("tensor", "sSC", evf[hc_ % 2])
                    for kt in range(DT):
                        op("tensor", lambda e, hc_=hc_, kt=kt, bank=bank: e.matmul(
                            bank[:, :], v3(w_q1, DT, H)[:, kt, hc_ * 128:(hc_ + 1) * 128],
                            v3(f_xq, DT, TC)[:, kt, :], start=(kt == 0), stop=(kt == DT - 1)))
                    vpe = tag("tensor", "sPE")
                    w8("scalar", "sPE", vpe)
                    if hc_ % 8 == 0:
                        w8("scalar", "sST", stp if hc_ == 0 else ctr["sST"])
                    op("scalar", lambda e, hc_=hc_, bank=bank: e.activation(
                        out=v3(f_h, 8, TC)[:, hc_ % 8, :], in_=bank[:, :],
                        func=AF.Gelu, scale=s_sc[:, 1:2], bias=s_b1s[:, hc_:hc_ + 1]))
                    evf[hc_ % 2] = tag("scalar", "sSC")
                    if hc_ % 8 == 7:
                        vsc_h = ctr["sSC"]
                        w8("vector", "sSC", vsc_h)
                        op("vector", lambda e: e.tensor_reduce(
                            a_stat[:, 0:8], v3(f_h, 8, TC)[:, :, :], AX.X, ALU.max,
                            apply_absolute_value=True))
                        dr("vector")
                        op("vector", lambda e: e.tensor_reduce(
                            a_stat[:, 8:9], a_stat[:, 0:8], AX.X, ALU.max))
                        dr("vector")
                        op("vector", lambda e: e.tensor_max(a_amax[:, 1:2], a_amax[:, 1:2], a_stat[:, 8:9]))
                        tag("vector", "sVE")
                        w8("sync", "sSC", vsc_h)
                        dma("sync", "sST",
                            bass.AP(h_hbm, (hc_ - 7) * 128 * TCORE + c * TC,
                                    [[TCORE, 128], [128 * TCORE, 8], [1, TC]]),
                            v3(f_h, 8, TC)[:, :, :])
                        if dbg_hff is not None:
                            dma("sync", "sST",
                                bass.AP(dbg_hff, (hc_ - 7) * 128 * TCORE + c * TC,
                                        [[TCORE, 128], [128 * TCORE, 8], [1, TC]]),
                                v3(f_h, 8, TC)[:, :, :])
                            dbg_quiesce()
                        vpe_ffn_prev = ctr["sPE"]

            global_scale(a_amax[:, 1:2], ar2_in, ar2_out, 2, 3)
            op("vector", lambda e: e.tensor_mul(s_sc[:, 3:4], s_sc[:, 7:8], s_sc[:, 3:4]))
            dr("vector")
            op("vector", lambda e: e.tensor_scalar_mul(s_b2s[:, :], c_b2[:, :], s_sc[:, 3:4]))
            tag("vector", "sVE")
            if dbg_sc is not None:
                w8("sync", "sVE")
                dma("sync", "sST", dbg_sc[:, :], s_sc[:, :])
            if upto == "ffn1":
                raise _StopBuild

            phase_begin()
            s_dt = psbuf("s_dt", [128, 2 * TC])
            s_xc = psbuf("s_xc", [128, 2 * TC])
            s_dtx = psbuf("s_dtx", [128, 2 * TC])
            s_bc2 = psbuf("s_bc2", [64, 2 * TC])
            s_bchl = psbuf("s_bchl", [64, 2 * TC], BF16)
            s_dA = psbuf("s_dA", [128, 2 * TC])
            s_u = psbuf("s_u", [128, 2 * TC])
            s_hint = psbuf("s_hint", [128, N * TC])
            s_prod = psbuf("s_prod", [128, N * TC])
            s_y = psbuf("s_y", [128, TC])

            seg_done = {}
            segi = 0
            for hh in range(2):
                for c2 in range(CH):
                    for b_ in range(4):
                        slot = 2 * b_ + hh
                        sp_ = segi % 2
                        o_ = sp_ * TC
                        w8("sync", "sCC", min(2 * (c2 + 1), vcc_a2))
                        ch_ = "sLD" if sp_ == 0 else "sLE"
                        if sp_ in seg_done:
                            w8("sync", "sSC", seg_done[sp_][0])
                            w8("sync", "sVE", seg_done[sp_][1])
                        dma("sync", ch_, s_dt[:, o_:o_ + TC], a2a_out[c2][slot, 0, :, :])
                        dma("sync", ch_, s_xc[:, o_:o_ + TC], a2a_out[c2][slot, 1, :, :])
                        dma("sync", ch_, s_bc2[0:32, o_:o_ + TC], bc_out[c2][slot, :, :])
                        vbc = dma("sync", ch_, s_bc2[32:64, o_:o_ + TC], bc_out[c2][slot, :, :])
                        w8("vector", ch_, vbc)
                        op("vector", lambda e, o_=o_: e.tensor_mul(
                            s_dtx[:, o_:o_ + TC], s_dt[:, o_:o_ + TC], s_xc[:, o_:o_ + TC]))
                        op("vector", lambda e, o_=o_: e.tensor_copy(
                            out=s_bchl[0:32, o_:o_ + TC], in_=s_bc2[0:32, o_:o_ + TC]))
                        op("vector", lambda e, o_=o_: e.tensor_copy(
                            out=s_bchl[32:64, o_:o_ + TC], in_=s_bc2[32:64, o_:o_ + TC]))
                        op("vector", lambda e, o_=o_: e.tensor_sub(
                            s_bchl[32:64, o_:o_ + TC], s_bc2[32:64, o_:o_ + TC], s_bchl[32:64, o_:o_ + TC]))
                        vve_pre = tag("vector", "sVE")
                        vu = {}
                        vscan = {}
                        vprod = {}
                        for n_ in range(N):
                            par = n_ % 2
                            pbB = pb[4 + par]
                            pbC = pb[6 + par]
                            w8("tensor", "sVE", vu.get(par, vve_pre))
                            op("tensor", lambda e, n_=n_, pbB=pbB, o_=o_: e.matmul(
                                pbB[:, :], c_sel[:, n_ * 128:(n_ + 1) * 128], s_bchl[:, o_:o_ + TC],
                                start=True, stop=True))
                            w8("tensor", "sVE", vprod.get(par, vve_pre))
                            op("tensor", lambda e, n_=n_, pbC=pbC, o_=o_: e.matmul(
                                pbC[:, :], c_sel[:, (16 + n_) * 128:(17 + n_) * 128], s_bchl[:, o_:o_ + TC],
                                start=True, stop=True))
                            vpe_n = tag("tensor", "sPE")
                            w8("scalar", "sVE", vscan.get(par, vve_pre))
                            op("scalar", lambda e, n_=n_, par=par, o_=o_: e.activation(
                                out=s_dA[:, par * TC:(par + 1) * TC], in_=s_dt[:, o_:o_ + TC],
                                func=AF.Exp, scale=c_negA[:, n_:n_ + 1]))
                            vscal = tag("scalar", "sSC")
                            w8("vector", "sPE", vpe_n)
                            op("vector", lambda e, par=par, pbB=pbB, o_=o_: e.tensor_mul(
                                s_u[:, par * TC:(par + 1) * TC], s_dtx[:, o_:o_ + TC], pbB[:, :]))
                            vu[par] = tag("vector", "sVE")
                            w8("vector", "sSC", vscal)
                            op("vector", lambda e, n_=n_, b_=b_, par=par: e.tensor_tensor_scan(
                                out=bass.AP(s_hint, n_, [[N * TC, 128], [N, TC]]),
                                data0=s_dA[:, par * TC:(par + 1) * TC],
                                data1=s_u[:, par * TC:(par + 1) * TC],
                                initial=s_state[:, b_ * N + n_: b_ * N + n_ + 1],
                                op0=ALU.mult, op1=ALU.add))
                            vscan[par] = tag("vector", "sVE")
                            op("vector", lambda e, n_=n_, pbC=pbC: e.tensor_mul(
                                v3(s_prod, TC, N)[:, :, n_:n_ + 1],
                                v3(s_hint, TC, N)[:, :, n_:n_ + 1],
                                pbC[:, :]))
                            vprod[par] = tag("vector", "sVE")
                        op("vector", lambda e, b_=b_: e.tensor_copy(
                            out=s_state[:, b_ * N:(b_ + 1) * N],
                            in_=s_hint[:, (TC - 1) * N: TC * N]))
                        w8("vector", "sST")
                        op("vector", lambda e: e.tensor_reduce(
                            s_y[:, :], v3(s_prod, TC, N)[:, :, :], AX.X, ALU.add))
                        vve_y = tag("vector", "sVE")
                        seg_done[sp_] = (ctr["sSC"], vve_y)
                        segi += 1
                        w8("sync", "sVE", vve_y)
                        dma("sync", "sST", ya_in[c2][slot, :, :], s_y[:, :])
                    if hh == 1:
                        vst = ctr["sST"]
                        w8("gpsimd", "sST", vst)
                        ctr["sCC"] += 1
                        op("gpsimd", lambda e, c2=c2: e.collective_compute(
                            "AllToAll", ALU.bypass, ins=[ya_in[c2][:, :, :]],
                            outs=[ya_out[c2][:, :, :]], replica_groups=[list(range(8))]).then_inc(sems["sCC"], 1))

            vcc_scan = ctr["sCC"]
            if upto == "scan":
                raise _StopBuild

            phase_begin()
            p_yfm = psbuf("p_yfm", [128, DT * TC])
            p_aux = psbuf("p_aux", [128, DT * TC])
            p_gate = psbuf("p_gate", [128, DT * TC])
            p_xn = psbuf("p_xn", [128, DT * TC])
            p_res = psbuf("p_res", [128, DT * TC])
            p_tmp = psbuf("p_tmp", [128, DT * TC])
            p_mu = psbuf("p_mu", [128, TC])
            p_rs = psbuf("p_rs", [128, TC])
            w_o = psbuf("w_o", [128, DT * D])

            vWo = dma("sync", "sWB", w_o[:, :],
                      bass.AP(W_out, 0, [[D, 128], [128 * D, DT], [1, D]]))
            for c in range(CH):
                w8("sync", "sCC", vcc_scan - (CH - 1 - c))
                w8("sync", "sSC")
                w8("sync", "sVE")
                w8("sync", "sST")
                dma("sync", "sLD", v3(p_yfm, DT, TC)[:, :, :],
                    bass.AP(ya_out[c], 0, [[TC, 128], [128 * TC, 8], [1, TC]]))
                dma("sync", "sLD", v3(p_aux, DT, TC)[:, :, :],
                    bass.AP(a2a_in[c], 128 * TC, [[TC, 128], [2 * 128 * TC, DT], [1, TC]]))
                dma("sync", "sLD", v3(p_gate, DT, TC)[:, :, :],
                    bass.AP(gate_hbm, c * TC, [[TCORE, 128], [128 * TCORE, DT], [1, TC]]))
                vxn = dma("sync", "sLD", v3(p_xn, DT, TC)[:, :, :],
                          bass.AP(xn_hbm, c * TC, [[TCORE, 128], [128 * TCORE, DT], [1, TC]]))
                w8("vector", "sLD", vxn)
                for dt_ in range(DT):
                    op("vector", lambda e, dt_=dt_: e.scalar_tensor_tensor(
                        out=v3(p_tmp, DT, TC)[:, dt_, :], in0=v3(p_aux, DT, TC)[:, dt_, :],
                        scalar=c_D[:, dt_:dt_ + 1], in1=v3(p_yfm, DT, TC)[:, dt_, :],
                        op0=ALU.mult, op1=ALU.add))
                    op("vector", lambda e, dt_=dt_: e.tensor_mul(
                        v3(p_yfm, DT, TC)[:, dt_, :], v3(p_tmp, DT, TC)[:, dt_, :],
                        v3(p_gate, DT, TC)[:, dt_, :]))
                vve_y2 = tag("vector", "sVE")
                w8("tensor", "sVE", vve_y2)
                w8("tensor", "sWB", vWo)
                evo = {}
                for oc_ in range(DT):
                    bank = pb[oc_ % 2]
                    if oc_ >= 2:
                        w8("tensor", "sSC", evo[oc_ % 2])
                    for kt in range(DT):
                        op("tensor", lambda e, oc_=oc_, kt=kt, bank=bank: e.matmul(
                            bank[:, :], v3(w_o, DT, D)[:, kt, oc_ * 128:(oc_ + 1) * 128],
                            v3(p_yfm, DT, TC)[:, kt, :], start=(kt == 0), stop=(kt == DT - 1)))
                    vpe = tag("tensor", "sPE")
                    w8("scalar", "sPE", vpe)
                    op("scalar", lambda e, oc_=oc_, bank=bank: e.activation(
                        out=v3(p_tmp, DT, TC)[:, oc_, :], in_=bank[:, :],
                        func=AF.Identity, bias=c_bout[:, oc_:oc_ + 1]))
                    evo[oc_ % 2] = tag("scalar", "sSC")
                vsc_wo = ctr["sSC"]
                w8("vector", "sSC", vsc_wo)
                for dt_ in range(DT):
                    op("vector", lambda e, dt_=dt_: e.tensor_add(
                        v3(p_res, DT, TC)[:, dt_, :], v3(p_tmp, DT, TC)[:, dt_, :],
                        v3(p_xn, DT, TC)[:, dt_, :]))
                vve_res = tag("vector", "sVE")
                w8("scalar", "sVE", vve_res)
                for dt_ in range(DT):
                    op("scalar", lambda e, dt_=dt_: e.activation(
                        out=v3(p_aux, DT, TC)[:, dt_, :], in_=v3(p_res, DT, TC)[:, dt_, :],
                        func=AF.Square))
                vsc_sq = tag("scalar", "sSC")
                w8("tensor", "sSC", vsc_sq)
                for kt in range(DT):
                    op("tensor", lambda e, kt=kt: e.matmul(
                        pb[2][:, :], c_ones128[:, :], v3(p_res, DT, TC)[:, kt, :],
                        start=(kt == 0), stop=(kt == DT - 1)))
                for kt in range(DT):
                    op("tensor", lambda e, kt=kt: e.matmul(
                        pb[3][:, :], c_ones128[:, :], v3(p_aux, DT, TC)[:, kt, :],
                        start=(kt == 0), stop=(kt == DT - 1)))
                vpe_s = tag("tensor", "sPE")
                w8("vector", "sPE", vpe_s)
                op("vector", lambda e: e.tensor_scalar_mul(p_mu[:, :], pb[2][:, :], 1.0 / D))
                op("vector", lambda e: e.tensor_scalar_mul(p_rs[:, :], pb[3][:, :], 1.0 / D))
                dr("vector")
                op("vector", lambda e: e.tensor_mul(p_tmp[:, 0:TC], p_mu[:, :], p_mu[:, :]))
                dr("vector")
                op("vector", lambda e: e.tensor_sub(p_rs[:, :], p_rs[:, :], p_tmp[:, 0:TC]))
                dr("vector")
                op("vector", lambda e: e.tensor_scalar_add(p_rs[:, :], p_rs[:, :], LN_EPS))
                vve_st = tag("vector", "sVE")
                w8("scalar", "sVE", vve_st)
                op("scalar", lambda e: e.activation(
                    out=p_tmp[:, 0:TC], in_=p_rs[:, :], func=AF.Sqrt))
                vsc_st0 = tag("scalar", "sSC")
                w8("vector", "sSC", vsc_st0)
                op("vector", lambda e: e.reciprocal(p_rs[:, :], p_tmp[:, 0:TC]))
                vsc_st = tag("vector", "sVE")
                w8("vector", "sVE", vsc_st)
                for dt_ in range(DT):
                    op("vector", lambda e, dt_=dt_: e.tensor_sub(
                        v3(p_res, DT, TC)[:, dt_, :], v3(p_res, DT, TC)[:, dt_, :], p_mu[:, :]))
                    op("vector", lambda e, dt_=dt_: e.tensor_mul(
                        v3(p_res, DT, TC)[:, dt_, :], v3(p_res, DT, TC)[:, dt_, :], p_rs[:, :]))
                vve_n2 = tag("vector", "sVE")
                w8("scalar", "sVE", vve_n2)
                for dt_ in range(DT):
                    op("scalar", lambda e, dt_=dt_: e.activation(
                        out=v3(p_aux, DT, TC)[:, dt_, :], in_=v3(p_res, DT, TC)[:, dt_, :],
                        func=AF.Identity, scale=c_lng[:, dt_:dt_ + 1], bias=c_lnb[:, dt_:dt_ + 1]))
                vsc_m = tag("scalar", "sSC")
                w8("sync", "sSC", vsc_m)
                dma("sync", "sST", bass.AP(mamba_hbm, c * TC, [[TCORE, 128], [128 * TCORE, DT], [1, TC]]),
                    v3(p_aux, DT, TC)[:, :, :])
                if dbg_mam is not None:
                    dma("sync", "sST", bass.AP(dbg_mam, c * TC, [[TCORE, 128], [128 * TCORE, DT], [1, TC]]),
                        v3(p_aux, DT, TC)[:, :, :])
            if upto == "d1":
                raise _StopBuild

            phase_begin()
            g_mam = psbuf("g_mam", [128, DT * TC])
            g_tmp = psbuf("g_tmp", [128, DT * TC])
            g_ttm = psbuf("g_ttm", [128, 4 * D])
            g_xtm = psbuf("g_xtm", [128, 4 * D])
            g_hin = psbuf("g_hin", [128, 16 * TC])
            g_hq = psbuf("g_hq", [128, 16 * TC], BF16)
            g_i32 = psbuf("g_i32", [128, 4 * TC], I32)
            w_q2 = psbuf("w_q2", [128, HC * D], BF16)

            dbg_ffn = dbg("ffn", [DT, 128, TCORE])
            vwq2 = dma("sync", "sWB", w_q2[:, :],
                       bass.AP(w2qT_hbm, 0, [[D, 128], [128 * D, HC], [1, D]]))
            vhq = {}
            vq = {}
            prev = {}
            NG = HC // 4
            for c in range(CH):
                if c == 0:
                    w8("sync", "sSC")
                    w8("sync", "sVE")
                    w8("sync", "sST")
                for grp in range(2):
                    par = grp % 4
                    ch_ = ("sLD", "sLE", "sLF", "sLG")[par]
                    if par in vq:
                        w8("sync", "sVE", vq[par])
                    vh = dma("sync", ch_, g_hin[:, par * 4 * TC:(par + 1) * 4 * TC],
                             bass.AP(h_hbm, grp * 4 * 128 * TCORE + c * TC,
                                     [[TCORE, 128], [128 * TCORE, 4], [1, TC]]))
                    prev[("vh", grp)] = vh
                if c > 0:
                    w8("sync", "sVE", prev["vve_tot"])
                vm = dma("sync", "sWT", v3(g_mam, DT, TC)[:, :, :],
                         bass.AP(mamba_hbm, c * TC, [[TCORE, 128], [128 * TCORE, DT], [1, TC]]))
                if c > 0:
                    w8("sync", "sVE", prev["vve_o"])
                vxtm = dma("sync", "sWX", v3(g_xtm, 4, D)[:, :, :],
                           bass.AP(xx, (3 + c * TC) * D, [[D, 128], [128 * D, 4], [1, D]]))
                if c == 0:
                    w8("tensor", "sWB", vwq2)
                else:
                    w8("tensor", "sVE", prev["evt1"])
                    w8("tensor", "sSC", prev["vsc_f"])
                for grp in range(NG):
                    par = grp % 4
                    ch_ = ("sLD", "sLE", "sLF", "sLG")[par]
                    if grp < 2:
                        vh = prev[("vh", grp)]
                    else:
                        if par in vq:
                            w8("sync", "sVE", vq[par])
                        vh = dma("sync", ch_, g_hin[:, par * 4 * TC:(par + 1) * 4 * TC],
                                 bass.AP(h_hbm, grp * 4 * 128 * TCORE + c * TC,
                                         [[TCORE, 128], [128 * TCORE, 4], [1, TC]]))
                    w8("vector", ch_, vh)
                    w8("vector", "sPE", vhq.get(par, 0))
                    op("vector", lambda e, par=par: e.tensor_scalar_mul(
                        g_i32[:, :], g_hin[:, par * 4 * TC:(par + 1) * 4 * TC],
                        s_sc[:, 2:3]))
                    op("vector", lambda e, par=par: e.tensor_copy(
                        out=g_hq[:, par * 4 * TC:(par + 1) * 4 * TC], in_=g_i32[:, :]))
                    vve_hq = tag("vector", "sVE")
                    vq[par] = vve_hq
                    w8("tensor", "sVE", vve_hq)
                    for ki in range(4):
                        kt = grp * 4 + ki
                        for oc_ in range(DT):
                            op("tensor", lambda e, kt=kt, ki=ki, oc_=oc_, par=par: e.matmul(
                                pb[oc_][:, :], v3(w_q2, HC, D)[:, kt, oc_ * 128:(oc_ + 1) * 128],
                                g_hq[:, (par * 4 + ki) * TC:(par * 4 + ki + 1) * TC],
                                start=(kt == 0), stop=(kt == HC - 1)))
                    vhq[par] = tag("tensor", "sPE")
                vpe_f = ctr["sPE"]
                w8("scalar", "sPE", vpe_f)
                w8("scalar", "sWT", vm)
                for oc_ in range(DT):
                    op("scalar", lambda e, oc_=oc_: e.activation(
                        out=v3(g_tmp, DT, TC)[:, oc_, :], in_=pb[oc_][:, :],
                        func=AF.Identity, scale=s_sc[:, 3:4], bias=s_b2s[:, oc_:oc_ + 1]))
                vsc_f = tag("scalar", "sSC")
                prev["vsc_f"] = vsc_f
                if dbg_ffn is not None:
                    w8("sync", "sSC", vsc_f)
                    dma("sync", "sST", bass.AP(dbg_ffn, c * TC, [[TCORE, 128], [128 * TCORE, DT], [1, TC]]),
                        v3(g_tmp, DT, TC)[:, :, :])
                    dbg_quiesce()
                w8("vector", "sSC", vsc_f)
                for dt_ in range(DT):
                    op("vector", lambda e, dt_=dt_: e.tensor_add(
                        v3(g_tmp, DT, TC)[:, dt_, :], v3(g_tmp, DT, TC)[:, dt_, :],
                        v3(g_mam, DT, TC)[:, dt_, :]))
                vve_tot = tag("vector", "sVE")
                prev["vve_tot"] = vve_tot
                w8("tensor", "sVE", vve_tot)
                evt = {}
                w8("vector", "sST", prev.get("vst", 0))
                for g in range(8):
                    bank = pb[g % 2]
                    if g >= 2:
                        w8("tensor", "sVE", evt[g % 2])
                    for j in range(4):
                        dt_, tt = divmod(g * 4 + j, 4)
                        op("tensor", lambda e, bank=bank, j=j, tt=tt, dt_=dt_: e.transpose(
                            bank[:, j * 128:(j + 1) * 128],
                            v3(g_tmp, DT, TC)[:, dt_, tt * 128:(tt + 1) * 128], c_id[:, :]))
                    vpe = tag("tensor", "sPE")
                    w8("vector", "sPE", vpe)
                    for j in range(4):
                        dt_, tt = divmod(g * 4 + j, 4)
                        op("vector", lambda e, bank=bank, j=j, tt=tt, dt_=dt_: e.tensor_copy(
                            out=v3(g_ttm, 4, D)[:, tt, dt_ * 128:(dt_ + 1) * 128],
                            in_=bank[:, j * 128:(j + 1) * 128]))
                    evt[g % 2] = tag("vector", "sVE")
                prev["evt1"] = evt[1]
                w8("vector", "sWX", vxtm)
                for tt in range(4):
                    op("vector", lambda e, tt=tt: e.tensor_add(
                        v3(g_ttm, 4, D)[:, tt, :], v3(g_ttm, 4, D)[:, tt, :],
                        v3(g_xtm, 4, D)[:, tt, :]))
                vve_o = tag("vector", "sVE")
                prev["vve_o"] = vve_o
                w8("sync", "sVE", vve_o)
                dma("sync", "sST", bass.AP(out_t, c * TC * D, [[D, 128], [128 * D, 4], [1, D]]),
                    v3(g_ttm, 4, D)[:, :, :])
                prev["vst"] = ctr["sST"]

        try:
            _emit()
        except _StopBuild:
            pass
        if phase_es[0] is not None:
            phase_es[0].close()
            phase_es[0] = None
        w8("sync", "sST")
        w8("sync", "sLD")
        w8("sync", "sLE")

        with nc.Block() as block:
            @block.sync
            def _(eng):
                for f in ops["sync"]:
                    f(eng)

            @block.tensor
            def _(eng):
                for f in ops["tensor"]:
                    f(eng)

            @block.vector
            def _(eng):
                for f in ops["vector"]:
                    f(eng)

            @block.scalar
            def _(eng):
                for f in ops["scalar"]:
                    f(eng)

            @block.gpsimd
            def _(eng):
                with allow_ici_all_to_all():
                    for f in ops["gpsimd"]:
                        f(eng)
    finally:
        es.__exit__(None, None, None)

    return nc


_SEL_CACHE = {}


def _host_consts():
    if "sel" not in _SEL_CACHE:
        sel = np.zeros((64, 32 * 128), np.float32)
        for n_ in range(N):
            sel[n_, n_ * 128:(n_ + 1) * 128] = -1.0
            sel[32 + n_, n_ * 128:(n_ + 1) * 128] = -1.0
            sel[16 + n_, (16 + n_) * 128:(17 + n_) * 128] = 1.0
            sel[48 + n_, (16 + n_) * 128:(17 + n_) * 128] = 1.0
        _SEL_CACHE["sel"] = sel.astype(ml_dtypes.bfloat16)
        _SEL_CACHE["id"] = np.eye(128, dtype=np.float32)
        _SEL_CACHE["onesr"] = np.ones((1, 128), np.float32)
    return _SEL_CACHE


def make_in_maps(inputs):
    x = np.asarray(inputs["L0_attn_x"], np.float32)
    cst = _host_consts()
    common = dict(
        W_in=np.ascontiguousarray(np.asarray(inputs["L0_attn_W_in"], np.float32)),
        b_in=np.asarray(inputs["L0_attn_b_in"], np.float32).reshape(1, DH),
        conv_w=np.ascontiguousarray(np.asarray(inputs["L0_attn_conv_w"], np.float32).reshape(D, 4)),
        conv_b=np.asarray(inputs["L0_attn_conv_b"], np.float32).reshape(1, D),
        W_bc=np.ascontiguousarray(np.asarray(inputs["L0_attn_W_bc"], np.float32)),
        b_bc=np.asarray(inputs["L0_attn_b_bc"], np.float32).reshape(1, 2 * N),
        W_dt=np.ascontiguousarray(np.asarray(inputs["L0_attn_W_dt"], np.float32)),
        b_dt=np.asarray(inputs["L0_attn_b_dt"], np.float32).reshape(1, D),
        Dvec=np.asarray(inputs["L0_attn_D"], np.float32).reshape(1, D),
        W_out=np.ascontiguousarray(np.asarray(inputs["L0_attn_W_out"], np.float32)),
        b_out=np.asarray(inputs["L0_attn_b_out"], np.float32).reshape(1, D),
        ln_g=np.asarray(inputs["L0_attn_ln_g"], np.float32).reshape(1, D),
        ln_b=np.asarray(inputs["L0_attn_ln_b"], np.float32).reshape(1, D),
        w1=np.ascontiguousarray(np.asarray(inputs["L0_ffn_w1"], np.float32)),
        b1=np.asarray(inputs["L0_ffn_b1"], np.float32).reshape(1, H),
        w2=np.ascontiguousarray(np.asarray(inputs["L0_ffn_w2"], np.float32)),
        b2=np.asarray(inputs["L0_ffn_b2"], np.float32).reshape(1, D),
        ident=cst["id"], ones_r=cst["onesr"], selBC=cst["sel"],
    )
    A_log = np.asarray(inputs["L0_attn_A_log"], np.float32)
    in_maps = []
    for k in range(8):
        b_, hh = divmod(k, 2)
        base = hh * TCORE
        if hh == 0:
            lead = np.zeros((3, D), np.float32)
            lm = np.zeros((128, 1), np.float32)
        else:
            lead = x[b_, base - 3:base, :]
            lm = np.ones((128, 1), np.float32)
        xxk = np.concatenate([lead, x[b_, base:base + TCORE, :]], axis=0)
        m = dict(common)
        m["xx"] = np.ascontiguousarray(xxk)
        m["A_log_sl"] = np.ascontiguousarray(A_log[k * 128:(k + 1) * 128, :])
        m["leadmask"] = lm
        in_maps.append(m)
    return in_maps


def kernel(**inputs):
    in_maps = make_in_maps(inputs)
    nc = build_nc()
    res = run_bass_kernel_spmd(nc, in_maps, core_ids=list(range(8)))
    outs = np.stack([np.asarray(r["out"]) for r in res.results])
    return np.ascontiguousarray(outs.reshape(B, T, D).astype(np.float32))
